# version 01
# baseline (speedup 1.0000x reference)
import contextlib
import os
import numpy as np
import concourse.bass as bass
import concourse.mybir as mybir
from concourse.bass_utils import run_bass_kernel_spmd

F32 = mybir.dt.float32
BF16 = mybir.dt.bfloat16
AF = mybir.ActivationFunctionType
ALU = mybir.AluOpType
AX = mybir.AxisListType

NCORES = 8
T_GLOB = 8192
TLOC = 2048
D = 1024
NCH = D // 128
NBLK = TLOC // 512


ENGINES = ("tensor", "scalar", "vector", "gpsimd", "sync")

class _Op:
    __slots__ = ("idx", "engine", "emit", "reads", "writes", "kind", "tag",
                 "signal_idx", "waits")
    def __init__(self, idx, engine, emit, reads, writes, kind, tag):
        self.idx, self.engine, self.emit = idx, engine, emit
        self.reads, self.writes, self.kind, self.tag = reads, writes, kind, tag
        self.signal_idx = None
        self.waits = []

def _edge_needed(prod, cons):
    if prod.engine != cons.engine:
        return True
    if prod.kind != "compute" or cons.kind != "compute":
        return True
    return prod.engine != "tensor"

class Prog:
    def __init__(self, nc):
        self.nc = nc
        self.ops = []
        self.last_writer = {}
        self.readers = {}

    PARANOID = bool(os.environ.get("MK_PARANOID"))
    DMASER = bool(os.environ.get("MK_DMASER"))
    SERTEN = bool(os.environ.get("MK_SERTEN"))
    SERPAIR = os.environ.get("MK_SERPAIR", "")

    def add(self, engine, emit, reads=(), writes=(), kind="compute", tag=""):
        op = _Op(len(self.ops), engine, emit, tuple(reads), tuple(writes), kind, tag)
        if self.DMASER and kind == "dma":
            prev = getattr(self, "_last_dma", {}).get(engine)
            if prev is not None:
                op.waits.append(prev)
            self._last_dma = getattr(self, "_last_dma", {})
            self._last_dma[engine] = op
        if self.SERPAIR:
            other = self.SERPAIR
            lastof = getattr(self, "_lastof", {})
            self._lastof = lastof
            if engine == "tensor" and other in lastof:
                op.waits.append(lastof[other])
            if engine == other and "tensor" in lastof:
                op.waits.append(lastof["tensor"])
            lastof[engine] = op
        if self.SERTEN:
            if engine == "tensor" and self.ops:
                op.waits.append(self.ops[-1])
            elif engine != "tensor":
                lt = getattr(self, "_last_t", None)
                if lt is not None:
                    op.waits.append(lt)
            if engine == "tensor":
                self._last_t = op
        if self.PARANOID and self.ops:
            prev = self.ops[-1]
            op.waits.append(prev)
            for k in op.writes:
                self.last_writer[k] = op
                self.readers[k] = []
            self.ops.append(op)
            return op
        deps = {}
        for k in op.reads:
            w = self.last_writer.get(k)
            if w is not None:
                deps[w.idx] = w
        for k in op.writes:
            w = self.last_writer.get(k)
            if w is not None:
                deps[w.idx] = w
            for r in self.readers.get(k, ()):
                deps[r.idx] = r
        for d in deps.values():
            if d is not op and _edge_needed(d, op):
                op.waits.append(d)
        for k in op.writes:
            self.last_writer[k] = op
            self.readers[k] = []
        for k in op.reads:
            self.readers.setdefault(k, []).append(op)
        self.ops.append(op)
        return op

    def build(self, final_ops=None):
        nc = self.nc
        needed = set()
        for op in self.ops:
            if op.kind in ("dma", "cc"):
                needed.add(op.idx)
            for d in op.waits:
                needed.add(d.idx)
        if final_ops:
            for op in final_ops:
                needed.add(op.idx)
        counters = {}
        dma_seq = {}
        for op in self.ops:
            if op.idx in needed:
                if op.kind == "dma":
                    i = dma_seq.get(op.engine, 0)
                    dma_seq[op.engine] = i + 1
                    sk = ("dma", op.engine, i % 8)
                    counters[sk] = counters.get(sk, 0) + 1
                    op.signal_idx = (sk, counters[sk])
                else:
                    sk = ("cc" if op.kind == "cc" else "op", op.engine)
                    counters[sk] = counters.get(sk, 0) + 1
                    op.signal_idx = (sk, counters[sk])
        sems = {sk: nc.alloc_semaphore(name='s_' + '_'.join(str(x) for x in sk)) for sk in counters}
        self.sems = sems
        per_engine = {e: [o for o in self.ops if o.engine == e] for e in ENGINES}
        awaited = {e: {} for e in ENGINES}

        def run_engine(eng_handle, ename):
            aw = awaited[ename]
            for op in per_engine[ename]:
                for d in op.waits:
                    sk, n = d.signal_idx
                    val = n * (16 if sk[0] == "dma" else 1)
                    if aw.get(sk, -1) < val:
                        eng_handle.wait_ge(sems[sk], val)
                        aw[sk] = val
                inst = op.emit(eng_handle)
                if op.signal_idx is not None:
                    sk, n = op.signal_idx
                    inst.then_inc(sems[sk], 16 if sk[0] == "dma" else 1)

        with nc.Block() as block:
            @block.tensor
            def _(t):
                run_engine(t, "tensor")
            @block.scalar
            def _(s):
                run_engine(s, "scalar")
            @block.vector
            def _(v):
                run_engine(v, "vector")
            @block.gpsimd
            def _(g):
                run_engine(g, "gpsimd")
            @block.sync
            def _(sy):
                run_engine(sy, "sync")
                if final_ops:
                    aw = awaited["sync"]
                    for op in final_ops:
                        sk, n = op.signal_idx
                        val = n * (16 if sk[0] == "dma" else 1)
                        if aw.get(sk, -1) < val:
                            sy.wait_ge(sems[sk], val)
                            aw[sk] = val


def _psum_banks(ap):
    try:
        name = ap.tensor.name
    except AttributeError:
        return set()
    if name not in ("psS", "psO"):
        return set()
    import concourse.mybir as _mb
    esz = 2 if ap.dtype == BF16 else 4
    apl = [list(x) for x in ap.ap]
    rowstride = apl[0][0]
    col0 = ap.offset % rowstride if rowstride else ap.offset
    ext = 0
    for step, count in apl[1:]:
        ext += step * (count - 1)
    b0 = (col0 * esz) // 2048
    b1 = ((col0 + ext) * esz) // 2048
    return {(name, b) for b in range(b0, b1 + 1)}

def _audit(tag, aps_rd, aps_wr, reads, writes):
    need_r = set()
    for a in aps_rd:
        need_r |= _psum_banks(a)
    need_w = set()
    for a in aps_wr:
        need_w |= _psum_banks(a)
    have_r = {k for k in list(reads) + list(writes) if isinstance(k, tuple)
              and len(k) == 2 and k[0] in ("psS", "psO")}
    have_w = {k for k in writes if isinstance(k, tuple) and len(k) == 2
              and k[0] in ("psS", "psO")}
    miss_r = need_r - have_r
    miss_w = need_w - have_w
    if miss_r or miss_w:
        raise AssertionError(
            f"PSUM key audit failed at {tag}: reads missing {miss_r}, "
            f"writes missing {miss_w} (declared reads={reads} writes={writes})")


def dma(p, q, out, in_, reads, writes, tag=""):
    return p.add(q, lambda e: e.dma_start(out=out, in_=in_), reads=reads,
                 writes=writes, kind="dma", tag=tag)

def memset(p, eng, ap, val, writes, reads=(), tag=""):
    return p.add(eng, lambda e: e.memset(ap, val), reads=reads, writes=writes,
                 kind="unsafe", tag=tag)

def matmul(p, out, lhsT, rhs, start, stop, reads, writes, tag=""):
    _audit(tag, [lhsT, rhs], [out], reads, writes)
    return p.add("tensor",
                 lambda e: e.matmul(out, lhsT, rhs, start=start, stop=stop,
                                    skip_group_check=True),
                 reads=reads, writes=writes, tag=tag)

def transpose(p, out, in_, ident, reads, writes, tag=""):
    _audit(tag, [in_, ident], [out], reads, writes)
    return p.add("tensor", lambda e: e.transpose(out, in_, ident),
                 reads=reads, writes=writes, tag=tag)

def act(p, out, in_, func, reads, writes, bias=0.0, scale=1.0, accum_out=None, tag=""):
    _audit(tag, [in_], [out] + ([accum_out] if accum_out is not None else []), reads, writes)
    return p.add("scalar",
                 lambda e: e.activation(out, in_, func, bias=bias, scale=scale,
                                        accum_out=accum_out),
                 reads=reads, writes=writes, tag=tag)

def tt(p, out, a, b, op, reads, writes, eng="vector", tag=""):
    _audit(tag, [a, b], [out], reads, writes)
    return p.add(eng, lambda e: e.tensor_tensor(out, a, b, op),
                 reads=reads, writes=writes, tag=tag)

def ts(p, out, a, scalar, op, reads, writes, eng="vector", tag=""):
    _audit(tag, [a], [out], reads, writes)
    return p.add(eng, lambda e: e.tensor_single_scalar(out, a, scalar, op),
                 reads=reads, writes=writes, tag=tag)

def tscalar2(p, out, in0, s1, s2, op0, op1, reads, writes, tag=""):
    _audit(tag, [in0], [out], reads, writes)
    return p.add("vector",
                 lambda e: e.tensor_scalar(out, in0, s1, s2, op0, op1),
                 reads=reads, writes=writes, tag=tag)

def stt(p, out, in0, scalar, in1, op0, op1, reads, writes, tag=""):
    sc = [scalar] if not isinstance(scalar, float) else []
    _audit(tag, [in0, in1] + sc, [out], reads, writes)
    return p.add("vector",
                 lambda e: e.scalar_tensor_tensor(out, in0, scalar, in1, op0, op1),
                 reads=reads, writes=writes, tag=tag)

def vcopy(p, out, in_, reads, writes, eng="vector", tag=""):
    _audit(tag, [in_], [out], reads, writes)
    return p.add(eng, lambda e: e.tensor_copy(out, in_),
                 reads=reads, writes=writes, tag=tag)

def reduce(p, out, in_, op, reads, writes, negate=False, tag=""):
    return p.add("vector",
                 lambda e: e.tensor_reduce(out, in_, AX.X, op, negate=negate),
                 reads=reads, writes=writes, tag=tag)

def ttscan(p, out, d0, d1, init, op0, op1, reads, writes, tag=""):
    return p.add("vector",
                 lambda e: e.tensor_tensor_scan(out, d0, d1, init, op0, op1),
                 reads=reads, writes=writes, tag=tag)

def recip(p, out, in_, reads, writes, tag=""):
    _audit(tag, [in_], [out], reads, writes)
    return p.add("vector", lambda e: e.reciprocal(out, in_),
                 reads=reads, writes=writes, tag=tag)

def collective(p, kind, op, groups, ins, outs, reads, writes, tag=""):
    return p.add("gpsimd",
                 lambda e: e.collective_compute(kind, op, replica_groups=groups,
                                                ins=ins, outs=outs),
                 reads=reads, writes=writes, kind="cc", tag=tag)


def make_ctx(nc, es, resident_x=True):
    c = {"nc": nc}
    sb = lambda name, shape, dt: es.enter_context(nc.sbuf_tensor(name, shape, dt))
    ps = lambda name, shape, dt: es.enter_context(nc.psum_tensor(name, shape, dt))
    if resident_x:
        c["x"] = sb("x", [128, NCH, TLOC], F32)
    else:
        c["xs"] = sb("xs", [128, 2, TLOC], F32)
        c["xob"] = sb("xob", [128, 2, 512], F32)
    c["xn"] = sb("xn", [128, NCH, TLOC], BF16)
    c["psS"] = ps("psS", [128, 2048], F32)
    c["psO"] = ps("psO", [128, 2048], F32)
    c["ident"] = sb("ident", [128, 128], BF16)
    c["onescol"] = sb("onescol", [128, 1], BF16)
    c["onesrow"] = sb("onesrow", [1, 512], BF16)
    c["onesf32"] = sb("onesf32", [1, 128], F32)
    c["ones65"] = sb("ones65", [65, 128], F32)
    c["rowbuf"] = sb("rowbuf", [128, 2048], F32)
    c["cpool"] = sb("cpool", [128, 8], F32)
    c["cpool_vals"] = {}
    c["lnsq"] = sb("lnsq", [128, TLOC], BF16)
    c["lntmp"] = sb("lntmp", [128, TLOC], F32)
    return c

def cconst(p, c, val):
    vals = c["cpool_vals"]
    if val not in vals:
        col = len(vals)
        vals[val] = col
        memset(p, "gpsimd", c["cpool"][:, col:col + 1], val, writes=[("cpool", col)])
    col = vals[val]
    return c["cpool"][:, col:col + 1], ("cpool", col)

def load_ctx_constants(p, c, dident):
    dma(p, "gpsimd", c["ident"][:, :], dident[:, :], reads=(), writes=["ident"],
        tag="ld_ident")
    memset(p, "gpsimd", c["onescol"][:, :], 1.0, writes=["onescol"])
    memset(p, "gpsimd", c["onesrow"][:, :], 1.0, writes=["onesrow"])
    memset(p, "gpsimd", c["onesf32"][:, :], 1.0, writes=["onesf32"])
    memset(p, "gpsimd", c["ones65"][:, :], 1.0, writes=["aao1"])


def emit_ln(p, c, tag, xin=None):
    resident = "x" in c
    xn = c["xn"]
    if resident:
        x = c["x"]
    else:
        xs = c["xs"]
        xin_r = xin.rearrange("(c p) t -> p c t", p=128)
    psS, psO = c["psS"], c["psO"]
    lnsq, lntmp = c["lnsq"], c["lntmp"]
    rows = c["rowbuf"]
    onescol, onesf32 = c["onescol"], c["onesf32"]
    kS = [("psS", i) for i in range(4)]
    kO = [("psO", i) for i in range(4)]
    for ch in range(NCH):
        if resident:
            xch, kx = x[:, ch, :], ["x"]
        else:
            bb = ch % 2
            dma(p, "sync", xs[:, bb, :], xin_r[:, ch, :], reads=[],
                writes=[("xs", bb)], tag=f"{tag}.lx{ch}")
            xch, kx = xs[:, bb, :], [("xs", bb)]
        vcopy(p, xn[:, ch, :], xch, reads=kx, writes=[("xn", ch)],
              tag=f"{tag}.cast{ch}")
        act(p, lnsq[:, :], xch, AF.Square, reads=kx,
            writes=["lnsq"], tag=f"{tag}.sq{ch}")
        for nb in range(NBLK):
            sl = slice(nb * 512, (nb + 1) * 512)
            matmul(p, psS[0:1, sl], onescol[:, 0:1], xn[:, ch, sl],
                   start=(ch == 0), stop=(ch == NCH - 1),
                   reads=[("xn", ch), "onescol"], writes=[("psS", nb)],
                   tag=f"{tag}.msum{ch}.{nb}")
            matmul(p, psO[0:1, sl], onescol[:, 0:1], lnsq[:, sl],
                   start=(ch == 0), stop=(ch == NCH - 1),
                   reads=["lnsq", "onescol"], writes=[("psO", nb)],
                   tag=f"{tag}.msq{ch}.{nb}")
    r0 = rows[0:1, 0:2048]
    r3, r2, r0sq = lntmp[0:1, 0:2048], lntmp[64:65, 0:2048], lntmp[96:97, 0:2048]
    ts(p, r0, psS[0:1, :], -1.0 / D, ALU.mult, reads=kS, writes=["r0"], tag=f"{tag}.mu")
    tt(p, r0sq, r0, r0, ALU.mult, reads=["r0"], writes=["r0sq"], tag=f"{tag}.musq")
    stt(p, r2, psO[0:1, :], 1.0 / D, r0sq, ALU.mult, ALU.subtract,
        reads=kO + ["r0sq"], writes=["r2"], tag=f"{tag}.var")
    epsap, epsk = cconst(p, c, 1e-5)
    act(p, r2, r2, AF.Sqrt, bias=epsap[0:1, :], reads=["r2", epsk], writes=["r2"],
        tag=f"{tag}.sqrt")
    recip(p, r3, r2, reads=["r2"], writes=["r3"], tag=f"{tag}.rstd")
    for nb in range(NBLK):
        sl = slice(nb * 512, (nb + 1) * 512)
        matmul(p, psO[:, sl], onesf32[0:1, :], r0[0:1, sl], start=True, stop=True,
               reads=["r0", "onesf32"], writes=[("psO", nb)], tag=f"{tag}.bcb{nb}")
        matmul(p, psS[:, sl], onesf32[0:1, :], r3[0:1, sl], start=True, stop=True,
               reads=["r3", "onesf32"], writes=[("psS", nb)], tag=f"{tag}.bca{nb}")
    for ch in range(NCH):
        if resident:
            xch, kx = x[:, ch, :], ["x"]
        else:
            bb = ch % 2
            dma(p, "sync", xs[:, bb, :], xin_r[:, ch, :], reads=[],
                writes=[("xs", bb)], tag=f"{tag}.lx2{ch}")
            xch, kx = xs[:, bb, :], [("xs", bb)]
        tt(p, lntmp[:, :], xch, psO[:, :], ALU.add,
           reads=kx + kO, writes=["lntmp", "r3", "r2", "r0sq"], tag=f"{tag}.sub{ch}")
        tt(p, xn[:, ch, :], lntmp[:, :], psS[:, :], ALU.mult,
           reads=["lntmp"] + kS, writes=[("xn", ch)], tag=f"{tag}.nrm{ch}")


def emit_l0ffn(p, c, es, W, tag):
    nc = p.nc
    sb = lambda name, shape, dt: es.enter_context(nc.sbuf_tensor(name, shape, dt))
    if "ffn_H1" not in c:
        c["ffn_H1"] = sb("ffn_H1", [128, 32, 512], BF16)
        c["ffn_H2"] = sb("ffn_H2", [128, 32, 512], BF16)
        c["ffn_w"] = sb("ffn_w", [128, 2, 32, 128], BF16)
        c["ffn_g"] = sb("ffn_g", [128, 2, 512], BF16)
        c["ffn_sc"] = sb("ffn_sc", [128, 136], F32)
    H1, H2 = c["ffn_H1"], c["ffn_H2"]
    wbuf, gbuf, scales = c["ffn_w"], c["ffn_g"], c["ffn_sc"]
    x, xn, psS, psO = c["x"], c["xn"], c["psS"], c["psO"]
    dma(p, "sync", scales[:, 0:32], W["w1_s"], reads=(), writes=["ffn_s1"], tag=f"{tag}.ls1")
    dma(p, "sync", scales[:, 32:64], W["w2_s"], reads=(), writes=["ffn_s2"], tag=f"{tag}.ls2")
    dma(p, "sync", scales[:, 64:96], W["h1_s"], reads=(), writes=["ffn_sh1"], tag=f"{tag}.lsh1")
    dma(p, "sync", scales[:, 96:128], W["h2_s"], reads=(), writes=["ffn_sh2"], tag=f"{tag}.lsh2")
    dma(p, "sync", scales[:, 128:136], W["w3_s"], reads=(), writes=["ffn_s3"], tag=f"{tag}.ls3")
    s1 = lambda oc: scales[:, oc:oc + 1]
    s2 = lambda oc: scales[:, 32 + oc:33 + oc]
    sh1 = lambda oc: scales[:, 64 + oc:65 + oc]
    sh2 = lambda oc: scales[:, 96 + oc:97 + oc]
    s3 = lambda oc: scales[:, 128 + oc:129 + oc]

    for nq in range(4):
        tsl = slice(nq * 512, (nq + 1) * 512)
        for oc in range(32):
            bb = oc % 2
            psl = slice(bb * 512, (bb + 1) * 512)
            kw = (f"{tag}.wld", bb)
            dma(p, "gpsimd", wbuf[:, bb, 0:8, :], W["w1"][oc],
                reads=(), writes=[(kw, 0)], tag=f"{tag}.lw1.{nq}.{oc}")
            dma(p, "gpsimd", wbuf[:, bb, 8:16, :], W["w2"][oc],
                reads=(), writes=[(kw, 1)], tag=f"{tag}.lw2.{nq}.{oc}")
            for kc in range(8):
                matmul(p, psS[:, psl], wbuf[:, bb, kc, :], xn[:, kc, tsl],
                       start=(kc == 0), stop=(kc == 7),
                       reads=[(kw, 0), ("xn", kc)], writes=[("psS", bb)],
                       tag=f"{tag}.m1.{nq}.{oc}.{kc}")
                matmul(p, psO[:, psl], wbuf[:, bb, 8 + kc, :], xn[:, kc, tsl],
                       start=(kc == 0), stop=(kc == 7),
                       reads=[(kw, 1), ("xn", kc)], writes=[("psO", bb)],
                       tag=f"{tag}.m2.{nq}.{oc}.{kc}")
            act(p, gbuf[:, bb, :], psS[:, psl], AF.Silu, scale=s1(oc),
                reads=[("psS", bb), "ffn_s1"], writes=[("ffn_g", bb)],
                tag=f"{tag}.silu.{nq}.{oc}")
            stt(p, H1[:, oc, :], psO[:, psl], s2(oc), gbuf[:, bb, :],
                ALU.mult, ALU.mult,
                reads=[("psO", bb), "ffn_s2", ("ffn_g", bb)],
                writes=[("ffnH1", oc)], tag=f"{tag}.gate.{nq}.{oc}")
        for oc in range(32):
            bb = oc % 2
            psl = slice(bb * 512, (bb + 1) * 512)
            kw = (f"{tag}.wld", bb)
            dma(p, "gpsimd", wbuf[:, bb, :, :], W["h1"][oc],
                reads=(), writes=[(kw, 0), (kw, 1)], tag=f"{tag}.lh1.{nq}.{oc}")
            for kc in range(32):
                matmul(p, psS[:, psl], wbuf[:, bb, kc, :], H1[:, kc, :],
                       start=(kc == 0), stop=(kc == 31),
                       reads=[(kw, 0), (kw, 1), ("ffnH1", kc)],
                       writes=[("psS", bb)], tag=f"{tag}.mh1.{nq}.{oc}.{kc}")
            act(p, H2[:, oc, :], psS[:, psl], AF.Gelu, scale=sh1(oc),
                reads=[("psS", bb), "ffn_sh1"], writes=[("ffnH2", oc)],
                tag=f"{tag}.gelu1.{nq}.{oc}")
        for oc in range(32):
            bb = oc % 2
            psl = slice(bb * 512, (bb + 1) * 512)
            kw = (f"{tag}.wld", bb)
            dma(p, "gpsimd", wbuf[:, bb, :, :], W["h2"][oc],
                reads=(), writes=[(kw, 0), (kw, 1)], tag=f"{tag}.lh2.{nq}.{oc}")
            for kc in range(32):
                matmul(p, psS[:, psl], wbuf[:, bb, kc, :], H2[:, kc, :],
                       start=(kc == 0), stop=(kc == 31),
                       reads=[(kw, 0), (kw, 1), ("ffnH2", kc)],
                       writes=[("psS", bb)], tag=f"{tag}.mh2.{nq}.{oc}.{kc}")
            act(p, H1[:, oc, :], psS[:, psl], AF.Gelu, scale=sh2(oc),
                reads=[("psS", bb), "ffn_sh2"], writes=[("ffnH1", oc)],
                tag=f"{tag}.gelu2.{nq}.{oc}")
        for oc in range(8):
            bb = oc % 2
            psl = slice(bb * 512, (bb + 1) * 512)
            kw = (f"{tag}.wld", bb)
            dma(p, "gpsimd", wbuf[:, bb, :, :], W["w3"][oc],
                reads=(), writes=[(kw, 0), (kw, 1)], tag=f"{tag}.lw3.{nq}.{oc}")
            for kc in range(32):
                matmul(p, psO[:, psl], wbuf[:, bb, kc, :], H1[:, kc, :],
                       start=(kc == 0), stop=(kc == 31),
                       reads=[(kw, 0), (kw, 1), ("ffnH1", kc)],
                       writes=[("psO", bb)], tag=f"{tag}.mw3.{nq}.{oc}.{kc}")
            stt(p, x[:, oc, tsl], psO[:, psl], s3(oc), x[:, oc, tsl],
                ALU.mult, ALU.add,
                reads=[("psO", bb), "ffn_s3", "x"], writes=["x"],
                tag=f"{tag}.res.{nq}.{oc}")


def emit_l1moe(p, c, es, W, tag):
    nc = p.nc
    sb = lambda name, shape, dt: es.enter_context(nc.sbuf_tensor(name, shape, dt))
    if "moe_H" not in c:
        c["moe_H"] = sb("moe_H", [128, 16, 4, 512], BF16)
        c["moe_wgu"] = sb("moe_wgu", [128, 2, 2, 4, 8, 128], BF16)
        c["moe_wd"] = sb("moe_wd", [128, 1, 8, 4, 128], BF16)
        c["moe_rw"] = sb("moe_rw", [128, 8, 16], BF16)
        c["moe_sc"] = sb("moe_sc", [128, 16, 16], F32)
        c["moe_rs"] = sb("moe_rs", [128, 128], F32)
        c["moe_c16"] = sb("moe_c16", [128, 16], BF16)
        c["moe_cf"] = sb("moe_cf", [16, 2048], BF16)
        c["moe_cb"] = sb("moe_cb", [128, 512], BF16)
        c["moe_g"] = sb("moe_g", [128, 4, 512], BF16)
        c["moe_t"] = sb("moe_t", [128, 512], BF16)
        c["moe_selm"] = sb("moe_selm", [16, 2048], BF16)
        c["moe_rb"] = sb("moe_rb", [128, 16], F32)
        c["moe_idf"] = sb("moe_idf", [128, 128], F32)
        if os.environ.get("MK_DUMP"):
            c["moe_dbg"] = sb("moe_dbg", [128, 3, 128], F32)
    H, wgu, wd = c["moe_H"], c["moe_wgu"], c["moe_wd"]
    stm, rs, c16 = c["moe_sc"], c["moe_rs"], c["moe_c16"]
    cf, cbs, gsb, tmb = c["moe_cf"], c["moe_cb"], c["moe_g"], c["moe_t"]
    x, xn, psS, psO = c.get("x"), c["xn"], c["psS"], c["psO"]
    rows, lntmp = c["rowbuf"], c["lntmp"]
    kS = [("psS", i) for i in range(4)]
    kSF = ["lntmp", "r3", "r2", "r0sq"]
    scores_fm = lntmp[0:16, 0:2048]
    NEG = -1.0e9

    if not c.get("moe_consts_loaded"):
        c["moe_consts_loaded"] = True
        dma(p, "gpsimd", c["moe_selm"][:, :], W["selm"], reads=(), writes=["selm"],
            tag=f"{tag}.lselm")
        dma(p, "sync", c["moe_idf"][:, :], W["identf"], reads=(), writes=["idf"],
            tag=f"{tag}.lidf")
    dma(p, "gpsimd", c["moe_rw"][:, :, :], W["rwT"], reads=(), writes=["moerw"],
        tag=f"{tag}.lrw")
    dma(p, "sync", rows[0:1, 0:16], W["rbias"], reads=(), writes=["rbrow"],
        tag=f"{tag}.lrb")
    matmul(p, psO[:, 0:16], c["onesf32"][0:1, :], rows[0:1, 0:16], start=True,
           stop=True, reads=["rbrow", "onesf32"], writes=[("psO", 0)], tag=f"{tag}.rbb")
    vcopy(p, c["moe_rb"][:, :], psO[:, 0:16], reads=[("psO", 0)], writes=["rbb"],
          tag=f"{tag}.rbc")
    for nb in range(NBLK):
        sl = slice(nb * 512, (nb + 1) * 512)
        for kc in range(NCH):
            matmul(p, psS[0:16, sl], c["moe_rw"][:, kc, :], xn[:, kc, sl],
                   start=(kc == 0), stop=(kc == 7),
                   reads=["moerw", ("xn", kc)], writes=[("psS", nb)],
                   tag=f"{tag}.rl.{nb}.{kc}")
    act(p, scores_fm, psS[0:16, :], AF.Sigmoid, reads=kS, writes=kSF, tag=f"{tag}.sig")
    for ttl in range(16):
        transpose(p, psO[0:128, 512:528], scores_fm[:, ttl * 128:(ttl + 1) * 128],
                  c["moe_idf"][0:16, 0:16], reads=kSF + ["idf"],
                  writes=[("psO", 1)], tag=f"{tag}.tr{ttl}")
        vcopy(p, stm[:, ttl, :], psO[:, 512:528], reads=[("psO", 1)],
              writes=[("sctm", ttl)], tag=f"{tag}.trc{ttl}")
    sc, eq, msk, sel, t3 = (rs[:, 0:16], rs[:, 16:32], rs[:, 32:48], rs[:, 48:64],
                            rs[:, 64:80])
    m1g, m2g, gsc, gmk = rs[:, 80:84], rs[:, 84:88], rs[:, 88:92], rs[:, 92:96]
    g1, g2, wsum, rden = rs[:, 96:97], rs[:, 97:98], rs[:, 98:99], rs[:, 99:100]
    cmb = rs[:, 100:116]
    sch8 = rs[:, 104 + 12:124]
    def pairred(pfx, dst, srcbase, n, op, RD, WR):
        cur, w = srcbase, n
        step = 0
        while w > 1:
            h = w // 2
            out = rs[:, 116:116 + h] if h > 1 else dst
            tt(p, out, rs[:, cur:cur + h], rs[:, cur + h:cur + w], op,
               reads=RD, writes=WR, tag=pfx + f".pr{step}")
            cur, w = 116, h
            step += 1
    g4 = lambda a: a.rearrange("p (g f) -> p g f", g=4)
    for ttl in range(16):
        s_t = stm[:, ttl, :]
        RD = [("sctm", ttl), "rs", "rbb"]
        WR = ["rs"]
        tg = f"{tag}.rt{ttl}"
        tt(p, sc, s_t, c["moe_rb"][:, :], ALU.add, reads=RD, writes=WR, tag=tg + ".sc")
        tt(p, m1g, rs[:, 0:16:4], rs[:, 1:16:4], ALU.max, reads=RD, writes=WR, tag=tg + ".a")
        tt(p, gmk, rs[:, 2:16:4], rs[:, 3:16:4], ALU.max, reads=RD, writes=WR, tag=tg + ".b")
        tt(p, m1g, m1g, gmk, ALU.max, reads=RD, writes=WR, tag=tg + ".c")
        tt(p, g4(eq), g4(sc), m1g.rearrange("p (g o) -> p g o", g=4).broadcast_to([128, 4, 4]),
           ALU.is_equal, reads=RD, writes=WR, tag=tg + ".eq1")
        stt(p, msk, eq, NEG, sc, ALU.mult, ALU.add, reads=RD, writes=WR, tag=tg + ".mk1")
        tt(p, m2g, rs[:, 32:48:4], rs[:, 33:48:4], ALU.max, reads=RD, writes=WR, tag=tg + ".d")
        tt(p, gmk, rs[:, 34:48:4], rs[:, 35:48:4], ALU.max, reads=RD, writes=WR, tag=tg + ".e")
        tt(p, m2g, m2g, gmk, ALU.max, reads=RD, writes=WR, tag=tg + ".f")
        tt(p, gsc, m1g, m2g, ALU.add, reads=RD, writes=WR, tag=tg + ".gsc")
        pairred(tg + ".g1", g1, 88, 4, ALU.max, RD, WR)
        tt(p, gmk, gsc, g1.broadcast_to([128, 4]), ALU.is_equal, reads=RD, writes=WR,
           tag=tg + ".eg1")
        stt(p, m2g, gmk, NEG, gsc, ALU.mult, ALU.add, reads=RD, writes=WR, tag=tg + ".gm")
        pairred(tg + ".g2", g2, 84, 4, ALU.max, RD, WR)
        tt(p, gmk, gsc, g2.broadcast_to([128, 4]), ALU.is_ge, reads=RD, writes=WR,
           tag=tg + ".ge")
        tt(p, g4(msk), g4(sc), gmk.rearrange("p (g o) -> p g o", g=4).broadcast_to([128, 4, 4]),
           ALU.mult, reads=RD, writes=WR, tag=tg + ".am")
        ts(p, m1g, gmk, -1.0, ALU.add, reads=RD, writes=WR, tag=tg + ".am1")
        stt(p, g4(msk), m1g.rearrange("p (g o) -> p g o", g=4).broadcast_to([128, 4, 4]),
            1.0e9, g4(msk), ALU.mult, ALU.add, reads=RD, writes=WR, tag=tg + ".m9")
        if os.environ.get("MK_DUMP") and ttl == 15 and "moe_dbg" in c:
            vcopy(p, c["moe_dbg"][:, 0, :], rs[:, :], reads=["rs"], writes=["dbg"],
                  tag=tg + ".snap0")
        for r in range(4):
            pairred(tg + f".mr{r}", g1, 32, 16, ALU.max, RD, WR)
            if os.environ.get("MK_DUMP") and ttl == 15 and "moe_dbg" in c and r == 0:
                vcopy(p, c["moe_dbg"][:, 1, :], rs[:, :], reads=["rs"], writes=["dbg"],
                      tag=tg + ".snap1")
            tt(p, eq, msk, g1.broadcast_to([128, 16]), ALU.is_equal, reads=RD,
               writes=WR, tag=tg + f".er{r}")
            if os.environ.get("MK_DUMP") and ttl == 15 and "moe_dbg" in c and r == 0:
                vcopy(p, c["moe_dbg"][:, 2, :], rs[:, :], reads=["rs"], writes=["dbg"],
                      tag=tg + ".snap2")
            stt(p, msk, eq, -2.0e9, msk, ALU.mult, ALU.add, reads=RD, writes=WR,
                tag=tg + f".ms{r}")
            if r == 0:
                vcopy(p, sel, eq, reads=RD, writes=WR, tag=tg + ".sl0")
            else:
                tt(p, sel, sel, eq, ALU.add, reads=RD, writes=WR, tag=tg + f".sl{r}")
        tt(p, t3, s_t, sel, ALU.mult, reads=RD, writes=WR, tag=tg + ".t3")
        pairred(tg + ".ws", wsum, 64, 16, ALU.add, RD, WR)
        ts(p, rden, wsum, 1.0e-20, ALU.add, reads=RD, writes=WR, tag=tg + ".ep")
        recip(p, rden, rden, reads=RD, writes=WR, tag=tg + ".rc")
        tscalar2(p, cmb, t3, rden[:, 0:1], 2.5, ALU.mult, ALU.mult, reads=RD,
                 writes=WR, tag=tg + ".cw")
        vcopy(p, c16[:, :], cmb, reads=["rs"], writes=["c16"], tag=tg + ".cc")
        transpose(p, psO[0:16, 528:592].bitcast(BF16), c16[:, :], c["ident"][:, :],
                  reads=["c16", "ident"], writes=[("psO", 1)], tag=tg + ".ct")
        vcopy(p, cf[:, ttl * 128:(ttl + 1) * 128], psO[0:16, 528:592].bitcast(BF16),
              reads=[("psO", 1)], writes=["cf"], tag=tg + ".cf")
    for q in range(4):
        tsl = slice(q * 512, (q + 1) * 512)
        for e in range(16):
            bb = e % 2
            kwg = (f"{tag}.wgu", bb)
            dma(p, "gpsimd", wgu[:, bb, 0, :, :, :],
                W["wg"][e].rearrange("o p k j -> p o k j"), reads=(),
                writes=[(kwg, 0)], tag=f"{tag}.lg.{q}.{e}")
            dma(p, "gpsimd", wgu[:, bb, 1, :, :, :],
                W["wu"][e].rearrange("o p k j -> p o k j"), reads=(),
                writes=[(kwg, 1)], tag=f"{tag}.lu.{q}.{e}")
            matmul(p, psS[:, 0:512], c["moe_selm"][:, e * 128:(e + 1) * 128],
                   cf[:, tsl], start=True, stop=True,
                   reads=["selm", "cf"], writes=[("psS", 0)], tag=f"{tag}.cb.{q}.{e}")
            act(p, cbs[:, :], psS[:, 0:512], AF.Identity, reads=[("psS", 0)],
                writes=["cb"], tag=f"{tag}.cbe.{q}.{e}")
            for hc in range(4):
                psl = slice(hc * 512, (hc + 1) * 512)
                for kc in range(8):
                    matmul(p, psS[:, psl], wgu[:, bb, 0, hc, kc, :], xn[:, kc, tsl],
                           start=(kc == 0), stop=(kc == 7),
                           reads=[(kwg, 0), ("xn", kc)], writes=[("psS", hc)],
                           tag=f"{tag}.mg.{q}.{e}.{hc}.{kc}")
                    matmul(p, psO[:, psl], wgu[:, bb, 1, hc, kc, :], xn[:, kc, tsl],
                           start=(kc == 0), stop=(kc == 7),
                           reads=[(kwg, 1), ("xn", kc)], writes=[("psO", hc)],
                           tag=f"{tag}.mu.{q}.{e}.{hc}.{kc}")
            for hc in range(4):
                psl = slice(hc * 512, (hc + 1) * 512)
                act(p, gsb[:, hc, :], psS[:, psl], AF.Silu, reads=[("psS", hc)],
                    writes=[("gsb", hc)], tag=f"{tag}.si.{q}.{e}.{hc}")
                tt(p, tmb[:, :], gsb[:, hc, :], psO[:, psl], ALU.mult,
                   reads=[("gsb", hc), ("psO", hc)], writes=["tmb"],
                   tag=f"{tag}.gu.{q}.{e}.{hc}")
                tt(p, H[:, e, hc, :], tmb[:, :], cbs[:, :], ALU.mult,
                   reads=["tmb", "cb"], writes=[("H", e, hc)],
                   tag=f"{tag}.hc.{q}.{e}.{hc}")
        for e in range(16):
            bb = 0
            kwd = (f"{tag}.wd", bb)
            dma(p, "gpsimd", wd[:, bb, :, :, :],
                W["wd"][e].rearrange("o p k j -> p o k j"), reads=(),
                writes=[kwd], tag=f"{tag}.ld.{q}.{e}")
            for ocb in range(8):
                pst = psS if ocb < 4 else psO
                psl = slice((ocb % 4) * 512, (ocb % 4 + 1) * 512)
                for kc in range(4):
                    matmul(p, pst[:, psl], wd[:, bb, ocb, kc, :], H[:, e, kc, :],
                           start=(e == 0 and kc == 0), stop=(e == 15 and kc == 3),
                           reads=[kwd, ("H", e, kc)],
                           writes=[("psS" if ocb < 4 else "psO", ocb % 4)],
                           tag=f"{tag}.md.{q}.{e}.{ocb}.{kc}")
        for ocb in range(8):
            pst = psS if ocb < 4 else psO
            psl = slice((ocb % 4) * 512, (ocb % 4 + 1) * 512)
            pk = ("psS" if ocb < 4 else "psO", ocb % 4)
            if "x" in c:
                tt(p, x[:, ocb, tsl], pst[:, psl], x[:, ocb, tsl], ALU.add,
                   reads=[pk, "x"], writes=["x"], tag=f"{tag}.res.{q}.{ocb}")
            else:
                bb = ocb % 2
                xin_r = c["xin_dram"].rearrange("(c p) t -> p c t", p=128)
                xout_r = c["xout_dram"].rearrange("(c p) t -> p c t", p=128)
                dma(p, "sync", c["xs"][:, bb, 0:512], xin_r[:, ocb, tsl], reads=[],
                    writes=[("xs", bb)], tag=f"{tag}.rlx.{q}.{ocb}")
                tt(p, c["xob"][:, bb, :], pst[:, psl], c["xs"][:, bb, 0:512], ALU.add,
                   reads=[pk, ("xs", bb)], writes=[("xob", bb)],
                   tag=f"{tag}.res.{q}.{ocb}")
                dma(p, "sync", xout_r[:, ocb, tsl], c["xob"][:, bb, :],
                    reads=[("xob", bb)], writes=[], tag=f"{tag}.rsx.{q}.{ocb}")

def prep_l1moe(inputs):
    W = {}
    wg, wu, wdn = inputs["L1_ffn_w_gate"], inputs["L1_ffn_w_up"], inputs["L1_ffn_w_down"]
    W["wg"] = np.ascontiguousarray(np.stack([_swz(wg[e].T) for e in range(16)]))
    W["wu"] = np.ascontiguousarray(np.stack([_swz(wu[e].T) for e in range(16)]))
    W["wd"] = np.ascontiguousarray(np.stack([_swz(wdn[e].T) for e in range(16)]))
    W["rwT"] = np.ascontiguousarray(
        inputs["L1_ffn_router_weight"].T.reshape(8, 128, 16).transpose(1, 0, 2))
    W["rbias"] = np.ascontiguousarray(inputs["L1_ffn_router_bias"][None, :].astype(np.float32))
    selm = np.zeros((16, 2048), np.float32)
    for e in range(16):
        selm[e, e * 128:(e + 1) * 128] = 1.0
    W["selm"] = selm
    W["identf"] = np.eye(128, dtype=np.float32)
    return W

MOE_W_SHAPES = {
    "wg": ([16, 4, 128, 8, 128], F32), "wu": ([16, 4, 128, 8, 128], F32),
    "wd": ([16, 8, 128, 4, 128], F32),
    "rwT": ([128, 8, 16], F32), "rbias": ([1, 16], F32),
    "selm": ([16, 2048], F32), "identf": ([128, 128], F32),
}


def emit_l0aa(p, c, es, W, tag):
    nc = p.nc
    sb = lambda name, shape, dt: es.enter_context(nc.sbuf_tensor(name, shape, dt))
    dr = lambda name, shape, dt: nc.dram_tensor(name, shape, dt).ap()
    if "aa_q" not in c:
        c["aa_q"] = sb("aa_q", [128, 8, 2048], BF16)
        c["aa_kch"] = sb("aa_kch", [128, 2, 2048], BF16)
        c["aa_vch"] = sb("aa_vch", [128, 2, 2048], BF16)
        c["aa_Kh"] = sb("aa_Kh", [128, 4, 2048], BF16)
        c["aa_Vf"] = sb("aa_Vf", [128, 4, 2048], BF16)
        c["aa_Vt"] = sb("aa_Vt", [128, 2, 64, 65], BF16)
        c["aa_eb"] = sb("aa_eb", [128, 2, 1024], BF16)
        c["aa_ob"] = sb("aa_ob", [128, 1024], F32)
        c["aa_cos"] = sb("aa_cos", [32, 2048], BF16)
        c["aa_sin"] = sb("aa_sin", [32, 2048], BF16)
        c["aa_np"] = sb("aa_np", [128, 8, 16], BF16)
        c["aa_w"] = sb("aa_w", [128, 2, 8, 128], BF16)
        c["aa_b"] = sb("aa_b", [1, 4, 1024], BF16)
        c["aa_m"] = sb("aa_m", [128, 16], F32)
        c["aa_idf"] = sb("aa_idf", [128, 128], F32)
        c["aa_rt"] = sb("aa_rt", [32, 2048], F32)
        c["aa_ksrc"] = dr("aa_ksrc", [16, 64, 2048], BF16)
        c["aa_vsrc"] = dr("aa_vsrc", [16, 64, 2048], BF16)
        c["aa_nsrc"] = dr("aa_nsrc", [16, 2048], F32)
        c["aa_kg"] = dr("aa_kg", [4, 16, 64, 2048], BF16)
        c["aa_vg"] = dr("aa_vg", [4, 16, 64, 2048], BF16)
        c["aa_ng"] = dr("aa_ng", [4, 16, 2048], F32)
    q, kch, vch = c["aa_q"], c["aa_kch"], c["aa_vch"]
    c["aa_nrm"] = c["rowbuf"][0:16, 0:2048]
    Kh, Vf, Vt, eb, ob = c["aa_Kh"], c["aa_Vf"], c["aa_Vt"], c["aa_eb"], c["aa_ob"]
    xn, psS, psO = c["xn"], c["psS"], c["psO"]
    y = xn
    rows, lntmp, lnsq = c["rowbuf"], c["lntmp"], c["lnsq"]
    kS = [("psS", i) for i in range(4)]
    kO = [("psO", i) for i in range(4)]
    COS, SIN = c["aa_cos"], c["aa_sin"]
    grp = [[0, 1, 2, 3], [4, 5, 6, 7]]
    SCL = 0.125

    dma(p, "gpsimd", COS[:, :], W["cos"], reads=(), writes=["cos"], tag=f"{tag}.lcos")
    dma(p, "gpsimd", SIN[:, :], W["sin"], reads=(), writes=["sin"], tag=f"{tag}.lsin")
    dma(p, "gpsimd", c["aa_np"][:, :, :], W["npat"], reads=(), writes=["npat"], tag=f"{tag}.lnp")
    dma(p, "gpsimd", c["aa_b"][:, :, :], W["biases"], reads=(), writes=["aab"], tag=f"{tag}.lb")
    dma(p, "sync", c["aa_idf"][:, :], W["identf"], reads=(), writes=["aaidf"], tag=f"{tag}.lidf")

    def rope(ps, kps, dstf, dkey, oc, which):
        t1, t2 = lntmp[0:32, :], c["aa_rt"][:, :]
        for half in range(2):
            b0 = half * 64
            x1s, x2s = slice(b0, b0 + 32), slice(b0 + 32, b0 + 64)
            tk = ("rope", which)
            tt(p, t1, ps[x1s, :], COS[:, :], ALU.mult, reads=kps + ["cos", "lntmp"],
               writes=[(tk, 1), "lntmp"], tag=f"{tag}.r1.{which}.{oc}.{half}")
            tt(p, t2, ps[x2s, :], SIN[:, :], ALU.mult, reads=kps + ["sin"],
               writes=[(tk, 2)], tag=f"{tag}.r2.{which}.{oc}.{half}")
            tt(p, dstf(x1s), t1, t2, ALU.subtract, reads=[(tk, 1), (tk, 2)],
               writes=[dkey], tag=f"{tag}.r3.{which}.{oc}.{half}")
            tt(p, t1, ps[x1s, :], SIN[:, :], ALU.mult, reads=kps + ["sin", "lntmp"],
               writes=[(tk, 1), "lntmp"], tag=f"{tag}.r4.{which}.{oc}.{half}")
            tt(p, t2, ps[x2s, :], COS[:, :], ALU.mult, reads=kps + ["cos"],
               writes=[(tk, 2)], tag=f"{tag}.r5.{which}.{oc}.{half}")
            tt(p, dstf(x2s), t1, t2, ALU.add, reads=[(tk, 1), (tk, 2)],
               writes=[dkey], tag=f"{tag}.r6.{which}.{oc}.{half}")

    def projchunk(which, wkey, ps, kp, oc, bslot, bb):
        kw = (f"{tag}.w", bb)
        dma(p, "gpsimd", c["aa_w"][:, bb, :, :], W[wkey][oc], reads=(),
            writes=[kw], tag=f"{tag}.l{which}.{oc}")
        for nb in range(NBLK):
            sl = slice(nb * 512, (nb + 1) * 512)
            for kc in range(NCH):
                matmul(p, ps[:, sl], c["aa_w"][:, bb, kc, :], xn[:, kc, sl],
                       start=(kc == 0), stop=False,
                       reads=[kw, ("xn", kc)], writes=[(kp, nb)],
                       tag=f"{tag}.m{which}.{oc}.{nb}.{kc}")
            matmul(p, ps[:, sl], c["aa_b"][0:1, bslot, oc * 128:(oc + 1) * 128],
                   c["onesrow"][0:1, 0:512], start=False, stop=True,
                   reads=["aab", "onesrow"], writes=[(kp, nb)],
                   tag=f"{tag}.mb{which}.{oc}.{nb}")

    def freemax(dst, dstkey, buf, width, keys, tg2):
        w = width
        step = 0
        while w > 1:
            h = w // 2
            o = buf[:, 0:h] if h > 1 else dst
            ww = keys if h > 1 else keys + [dstkey]
            tt(p, o, buf[:, 0:h], buf[:, h:2 * h], ALU.max, reads=keys,
               writes=ww, tag=tg2 + f".f{step}")
            if w % 2 == 1:
                tt(p, buf[:, 0:1], buf[:, 0:1], buf[:, w - 1:w], ALU.max,
                   reads=keys, writes=keys, tag=tg2 + f".fo{step}")
            w = h
            step += 1

    for oc in range(8):
        projchunk("q", "wq", psS, "psS", oc, 0, oc % 2)
        rope(psS, kS, lambda s, oc=oc: q[s, oc, :], ("aaq", oc), oc, "q")
        act(p, lnsq[:, :], q[:, oc, :], AF.Square, reads=[("aaq", oc)],
            writes=["lnsq"], tag=f"{tag}.sqq{oc}")
        for nb in range(NBLK):
            sl = slice(nb * 512, (nb + 1) * 512)
            matmul(p, psO[0:16, sl], c["aa_np"][:, oc, :], lnsq[:, sl],
                   start=(oc == 0), stop=(oc == 7),
                   reads=["lnsq", "npat"], writes=[("psO", nb)],
                   tag=f"{tag}.mnq.{oc}.{nb}")
    vcopy(p, c["aa_nrm"], psO[0:16, :], reads=kO, writes=["qn2", "r0"],
          tag=f"{tag}.eqn")
    freemax(rows[32:48, 4:5], "qm2x", c["aa_nrm"], 2048, ["qn2"],
            f"{tag}.qm")
    for oc in range(8):
        bb = oc % 2
        projchunk("k", "wk", psS, "psS", oc, 1, bb)
        rope(psS, kS, lambda s, bb=bb: kch[s, bb, :], ("aakc", bb), oc, "k")
        act(p, lnsq[:, :], kch[:, bb, :], AF.Square, reads=[("aakc", bb)],
            writes=["lnsq"], tag=f"{tag}.sqk{oc}")
        for nb in range(NBLK):
            sl = slice(nb * 512, (nb + 1) * 512)
            matmul(p, psO[0:16, sl], c["aa_np"][:, oc, :], lnsq[:, sl],
                   start=(oc == 0), stop=(oc == 7),
                   reads=["lnsq", "npat"], writes=[("psO", nb)],
                   tag=f"{tag}.mnk.{oc}.{nb}")
        for hh in range(2):
            rsl = slice(hh * 64, (hh + 1) * 64)
            dma(p, "gpsimd", c["aa_ksrc"][2 * oc + hh], kch[rsl, bb, :],
                reads=[("aakc", bb)], writes=["ksrc"], tag=f"{tag}.sk.{oc}.{hh}")
    vcopy(p, c["aa_nrm"], psO[0:16, :], reads=kO, writes=["kn2loc", "qn2"],
          tag=f"{tag}.ekn")
    dma(p, "gpsimd", c["aa_nsrc"], c["aa_nrm"], reads=["kn2loc"],
        writes=["nsrc"], tag=f"{tag}.skn")
    for oc in range(8):
        bb = oc % 2
        projchunk("v", "wv", psS, "psS", oc, 2, bb)
        act(p, vch[:, bb, :], psS[:, :], AF.Identity, reads=kS,
            writes=[("aavc", bb)], tag=f"{tag}.ev.{oc}")
        for hh in range(2):
            rsl = slice(hh * 64, (hh + 1) * 64)
            dma(p, "gpsimd", c["aa_vsrc"][2 * oc + hh], vch[rsl, bb, :],
                reads=[("aavc", bb)], writes=["vsrc"], tag=f"{tag}.sv.{oc}.{hh}")
    collective(p, "AllGather", ALU.bypass, grp, ins=[c["aa_ksrc"]],
               outs=[c["aa_kg"]], reads=["ksrc"], writes=["kg"], tag=f"{tag}.gk")
    collective(p, "AllGather", ALU.bypass, grp, ins=[c["aa_vsrc"]],
               outs=[c["aa_vg"]], reads=["vsrc"], writes=["vg"], tag=f"{tag}.gv")
    collective(p, "AllGather", ALU.bypass, grp, ins=[c["aa_nsrc"]],
               outs=[c["aa_ng"]], reads=["nsrc"], writes=["ng"], tag=f"{tag}.gn")
    for r in range(4):
        dma(p, "sync", c["aa_nrm"], c["aa_ng"][r], reads=["ng"],
            writes=["kn2s"] + (["kn2loc", "qn2"] if r == 0 else []),
            tag=f"{tag}.lkn{r}")
        freemax(rows[32:48, r:r + 1], "kmx", c["aa_nrm"], 2048,
                ["kn2s"], f"{tag}.km{r}")
    tt(p, rows[32:48, 0:1], rows[32:48, 0:1], rows[32:48, 1:2],
       ALU.max, reads=["kmx"], writes=["kmx"], tag=f"{tag}.kmA")
    tt(p, rows[32:48, 2:3], rows[32:48, 2:3], rows[32:48, 3:4],
       ALU.max, reads=["kmx"], writes=["kmx"], tag=f"{tag}.kmB")
    tt(p, rows[32:48, 0:1], rows[32:48, 0:1], rows[32:48, 2:3],
       ALU.max, reads=["kmx"], writes=["kmx"], tag=f"{tag}.kmC")
    tt(p, rows[32:48, 5:6], rows[32:48, 4:5], rows[32:48, 0:1],
       ALU.mult, reads=["kmx", "qm2x"], writes=["mh2"], tag=f"{tag}.mh2")
    act(p, rows[32:48, 6:7], rows[32:48, 5:6], AF.Sqrt,
        reads=["mh2"], writes=["mh"], tag=f"{tag}.mhs")
    transpose(p, psS[0:1, 0:16], rows[32:48, 6:7], c["aa_idf"][32:48, 32:48],
              reads=["mh", "aaidf"], writes=[("psS", 0)], tag=f"{tag}.mht")
    ts(p, rows[32:33, 1032:1048], psS[0:1, 0:16], -SCL * 1.005, ALU.mult,
       reads=[("psS", 0)], writes=["mhrow"], tag=f"{tag}.mhn")
    matmul(p, psS[:, 512:528], c["ones65"][32:33, 0:128], rows[32:33, 1032:1048],
           start=True, stop=True, reads=["mhrow", "aao1"], writes=[("psS", 1)],
           tag=f"{tag}.mhb")
    vcopy(p, c["aa_m"][:, :], psS[:, 512:528], reads=[("psS", 1)], writes=["aam"],
          tag=f"{tag}.mhc")

    for h in range(16):
        bb = h % 2
        base = bb * 64
        ch = h // 2
        kKh, kVf, kVt = ("Khb", bb), ("Vfb", bb), ("Vtb", bb)
        dma(p, "sync", Kh[base:base + 64, :, :],
            c["aa_kg"][:, h, :, :].rearrange("r d t -> d r t"),
            reads=["kg"], writes=[kKh], tag=f"{tag}.stk.{h}")
        dma(p, "sync", Vf[base:base + 64, :, :],
            c["aa_vg"][:, h, :, :].rearrange("r d t -> d r t"),
            reads=["vg"], writes=[kVf], tag=f"{tag}.stv.{h}")
        memset(p, "gpsimd", Vt[:, bb, :, 64:65], 1.0, writes=[kVt])
        for g in range(16):
            for j in range(4):
                kt = g * 4 + j
                src = Vf[base:base + 64, kt // 16, (kt % 16) * 128:(kt % 16) * 128 + 128]
                dstp = psO[:, 1024 + j * 32:1056 + j * 32].bitcast(BF16)
                transpose(p, dstp, src, c["ident"][base:base + 64, base:base + 64],
                          reads=[kVf, "ident"], writes=[("psO", 2)],
                          tag=f"{tag}.vt.{h}.{kt}")
            vcopy(p, Vt[:, bb, 4 * g:4 * g + 4, 0:64],
                  psO[:, 1024:1152].bitcast(BF16).rearrange("p (a b) -> p a b", a=4),
                  reads=[("psO", 2)], writes=[kVt], tag=f"{tag}.vte.{h}.{g}")
        for qh in range(2):
            for kt in range(64):
                sbf = kt % 2
                pss = slice(sbf * 1024, (sbf + 1) * 1024)
                kSb = [("psS", 2 * sbf), ("psS", 2 * sbf + 1)]
                lh = Kh[base:base + 64, kt // 16, (kt % 16) * 128:(kt % 16) * 128 + 128]
                for j in range(2):
                    matmul(p, psS[:, sbf * 1024 + j * 512: sbf * 1024 + (j + 1) * 512],
                           lh, q[base:base + 64, ch,
                                 qh * 1024 + j * 512:qh * 1024 + (j + 1) * 512],
                           start=True, stop=True,
                           reads=[kKh, ("aaq", ch)], writes=[kSb[j]],
                           tag=f"{tag}.s.{h}.{qh}.{kt}.{j}")
                act(p, eb[:, sbf, :], psS[:, pss], AF.Exp, bias=c["aa_m"][:, h:h + 1],
                    scale=SCL, reads=kSb + ["aam"], writes=[("eb", sbf)],
                    tag=f"{tag}.e.{h}.{qh}.{kt}")
                for j in range(2):
                    matmul(p, psO[0:65, j * 512:(j + 1) * 512], Vt[:, bb, kt, :],
                           eb[:, sbf, j * 512:(j + 1) * 512],
                           start=(kt == 0), stop=(kt == 63),
                           reads=[kVt, ("eb", sbf)], writes=[("psO", j)],
                           tag=f"{tag}.pv.{h}.{qh}.{kt}.{j}")
            recip(p, rows[64:65, 0:1024], psO[64:65, 0:1024],
                  reads=[("psO", 0), ("psO", 1)], writes=["rz"],
                  tag=f"{tag}.rz.{h}.{qh}")
            for j in range(2):
                matmul(p, psO[0:64, 1024 + j * 512:1536 + j * 512],
                       c["ones65"][64:65, 0:64], rows[64:65, j * 512:(j + 1) * 512],
                       start=True, stop=True, reads=["rz", "aao1"],
                       writes=[("psO", 2 + j)], tag=f"{tag}.rb.{h}.{qh}.{j}")
            vcopy(p, ob[0:64, :], psO[0:64, 0:1024], reads=[("psO", 0), ("psO", 1)],
                  writes=["aaob"], tag=f"{tag}.oe.{h}.{qh}")
            tt(p, y[base:base + 64, ch, qh * 1024:(qh + 1) * 1024], ob[0:64, :],
               psO[0:64, 1024:2048], ALU.mult,
               reads=["aaob", ("psO", 2), ("psO", 3)],
               writes=[("aay", ch), ("xn", ch)], tag=f"{tag}.ye.{h}.{qh}")
    xin_r = c["xin_dram"].rearrange("(c p) t -> p c t", p=128)
    xout_r = c["xout_dram"].rearrange("(c p) t -> p c t", p=128)
    for oc in range(8):
        bb = oc % 2
        kw = (f"{tag}.w", bb)
        dma(p, "gpsimd", c["aa_w"][:, bb, :, :], W["wo"][oc], reads=(),
            writes=[kw], tag=f"{tag}.lo.{oc}")
        for nb in range(NBLK):
            sl = slice(nb * 512, (nb + 1) * 512)
            for kc in range(NCH):
                matmul(p, psS[:, sl], c["aa_w"][:, bb, kc, :], y[:, kc, sl],
                       start=(kc == 0), stop=False,
                       reads=[kw, ("aay", kc)], writes=[("psS", nb)],
                       tag=f"{tag}.mo.{oc}.{nb}.{kc}")
            matmul(p, psS[:, sl], c["aa_b"][0:1, 3, oc * 128:(oc + 1) * 128],
                   c["onesrow"][0:1, 0:512], start=False, stop=True,
                   reads=["aab", "onesrow"], writes=[("psS", nb)],
                   tag=f"{tag}.mbo.{oc}.{nb}")
            dma(p, "sync", c["xs"][:, nb % 2, 0:512], xin_r[:, oc, sl], reads=[],
                writes=[("xs", nb % 2)], tag=f"{tag}.rlx.{oc}.{nb}")
            tt(p, c["xob"][:, nb % 2, :], psS[:, sl], c["xs"][:, nb % 2, 0:512],
               ALU.add, reads=[("psS", nb), ("xs", nb % 2)],
               writes=[("xob", nb % 2)], tag=f"{tag}.res.{oc}.{nb}")
            dma(p, "sync", xout_r[:, oc, sl], c["xob"][:, nb % 2, :],
                reads=[("xob", nb % 2)], writes=[], tag=f"{tag}.rsx.{oc}.{nb}")

def prep_l0aa(inputs, t0s):
    W = {}
    W["wq"] = _swz(inputs["L0_attn_a_wq_w"])
    W["wk"] = _swz(inputs["L0_attn_a_wk_w"])
    W["wv"] = _swz(inputs["L0_attn_a_wv_w"])
    W["wo"] = _swz(inputs["L0_attn_a_wo_w"])
    W["biases"] = np.ascontiguousarray(np.stack([
        inputs["L0_attn_a_wq_b"], inputs["L0_attn_a_wk_b"],
        inputs["L0_attn_a_wv_b"], inputs["L0_attn_a_wo_b"]])[None].astype(np.float32))
    npat = np.zeros((128, 8, 16), np.float32)
    for oc in range(8):
        npat[0:64, oc, 2 * oc] = 1.0
        npat[64:128, oc, 2 * oc + 1] = 1.0
    W["npat"] = npat
    W["identf"] = np.eye(128, dtype=np.float32)
    inv = 1.0 / (10000.0 ** (np.arange(32) / 32.0))
    W["cos_per_core"] = []
    W["sin_per_core"] = []
    for t0 in t0s:
        ang = np.outer(inv, np.arange(t0, t0 + TLOC, dtype=np.float64))
        W["cos_per_core"].append(np.cos(ang).astype(np.float32))
        W["sin_per_core"].append(np.sin(ang).astype(np.float32))
    return W

AA_W_SHAPES = {
    "wq": ([8, 128, 8, 128], F32), "wk": ([8, 128, 8, 128], F32),
    "wv": ([8, 128, 8, 128], F32), "wo": ([8, 128, 8, 128], F32),
    "biases": ([1, 4, 1024], F32), "npat": ([128, 8, 16], F32),
    "identf": ([128, 128], F32),
    "cos": ([32, 2048], F32), "sin": ([32, 2048], F32),
}


def emit_l0ab(p, c, es, W, tag):
    nc = p.nc
    sb = lambda name, shape, dt: es.enter_context(nc.sbuf_tensor(name, shape, dt))
    dr = lambda name, shape, dt: nc.dram_tensor(name, shape, dt).ap()
    if "ab_qn" not in c:
        c["ab_qn"] = sb("ab_qn", [128, 8, 2048], BF16)
        c["ab_kr"] = sb("ab_kr", [128, 8, 2048], BF16)
        c["ab_vch"] = sb("ab_vch", [128, 2, 2048], BF16)
        c["ab_Vf"] = sb("ab_Vf", [128, 4, 2048], BF16)
        c["ab_Vt"] = sb("ab_Vt", [128, 64, 2, 65], BF16)
        c["ab_eb"] = sb("ab_eb", [128, 2, 2, 512], BF16)
        c["ab_ein"] = sb("ab_ein", [128, 2, 512], F32)
        c["ab_g"] = sb("ab_g", [128, 512], BF16)
        c["ab_yc"] = sb("ab_yc", [128, 512], F32)
        c["ab_rb"] = sb("ab_rb", [128, 2, 512], F32)
        c["ab_rnb"] = sb("ab_rnb", [128, 2, 512], BF16)
        c["ab_w"] = sb("ab_w", [128, 2, 8, 128], BF16)
        c["ab_np"] = sb("ab_np", [128, 8, 16], BF16)
        c["ab_sel"] = sb("ab_sel", [16, 8, 128], F32)
        c["ab_sc"] = sb("ab_sc", [128, 18], F32)
        c["ab_bi"] = sb("ab_bi", [128, 8], F32)
        c["ab_row"] = sb("ab_row", [1, 256], F32)
        c["ab_ksrc"] = dr("ab_ksrc", [8, 128, 2048], BF16)
        c["ab_vsrc"] = dr("ab_vsrc", [8, 128, 2048], BF16)
        c["ab_kg"] = dr("ab_kg", [4, 8, 128, 2048], BF16)
        c["ab_vg"] = dr("ab_vg", [4, 8, 128, 2048], BF16)
    qn, kr, vch = c["ab_qn"], c["ab_kr"], c["ab_vch"]
    Vf, Vt, eb, ein = c["ab_Vf"], c["ab_Vt"], c["ab_eb"], c["ab_ein"]
    xn, psS, psO = c["xn"], c["psS"], c["psO"]
    y = xn
    rows, lntmp, lnsq = c["rowbuf"], c["lntmp"], c["lnsq"]
    arow = c["ab_row"]
    kS = [("psS", i) for i in range(4)]
    kO = [("psO", i) for i in range(4)]
    kLT = ["lntmp", "r3", "r2", "r0sq"]
    grp = [[0, 1, 2, 3], [4, 5, 6, 7]]

    dma(p, "gpsimd", c["ab_np"][:, :, :], W["npat"], reads=(), writes=["npat2"], tag=f"{tag}.lnp")
    dma(p, "gpsimd", c["ab_sel"][:, :, :], W["selpat"], reads=(), writes=["selp"], tag=f"{tag}.lsel")
    dma(p, "gpsimd", arow[0:1, 0:80], W["smalls"], reads=(), writes=["abrow"], tag=f"{tag}.lsm")
    dma(p, "gpsimd", c["ab_sc"][:, 9:10], W["swcol"], reads=(), writes=["absc"], tag=f"{tag}.lsw")
    dma(p, "gpsimd", rows[0:1, 0:256], W["lam"], reads=(), writes=["lamrow"], tag=f"{tag}.llam")
    tt(p, arow[0:1, 100:164], rows[0:1, 0:64], rows[0:1, 64:128], ALU.mult,
       reads=["lamrow"], writes=["lamt"], tag=f"{tag}.l1m")
    w = 64
    while w > 1:
        h = w // 2
        tt(p, arow[0:1, 100:100 + h], arow[0:1, 100:100 + h],
           arow[0:1, 100 + h:100 + w], ALU.add, reads=["lamt"], writes=["lamt"],
           tag=f"{tag}.l1r{w}")
        w = h
    tt(p, arow[0:1, 164:228], rows[0:1, 128:192], rows[0:1, 192:256], ALU.mult,
       reads=["lamrow"], writes=["lamt"], tag=f"{tag}.l2m")
    w = 64
    while w > 1:
        h = w // 2
        tt(p, arow[0:1, 164:164 + h], arow[0:1, 164:164 + h],
           arow[0:1, 164 + h:164 + w], ALU.add, reads=["lamt"], writes=["lamt"],
           tag=f"{tag}.l2r{w}")
        w = h
    act(p, arow[0:1, 100:101], arow[0:1, 100:101], AF.Exp, reads=["lamt"],
        writes=["lamt"], tag=f"{tag}.l1e")
    act(p, arow[0:1, 164:165], arow[0:1, 164:165], AF.Exp, reads=["lamt"],
        writes=["lamt"], tag=f"{tag}.l2e")
    tt(p, arow[0:1, 230:231], arow[0:1, 164:165], arow[0:1, 100:101], ALU.subtract,
       reads=["lamt"], writes=["lamt"], tag=f"{tag}.lsub")
    ts(p, arow[0:1, 230:231], arow[0:1, 230:231], -0.5, ALU.add, reads=["lamt"],
       writes=["lamt"], tag=f"{tag}.lad")
    act(p, arow[0:1, 96:97], arow[0:1, 0:1], AF.Sigmoid, reads=["abrow"],
        writes=["gsc"], tag=f"{tag}.sig")
    ts(p, arow[0:1, 96:97], arow[0:1, 96:97], 15.0, ALU.mult, reads=["gsc"],
       writes=["gsc"], tag=f"{tag}.g15")
    tt(p, arow[0:1, 88:92], arow[0:1, 1:5], arow[0:1, 5:9], ALU.add,
       reads=["abrow"], writes=["csr"], tag=f"{tag}.hm1")
    tt(p, arow[0:1, 88:90], arow[0:1, 88:90], arow[0:1, 90:92], ALU.add,
       reads=["csr"], writes=["csr"], tag=f"{tag}.hm2")
    tt(p, arow[0:1, 88:89], arow[0:1, 88:89], arow[0:1, 89:90], ALU.add,
       reads=["csr"], writes=["csr"], tag=f"{tag}.hm3")
    ts(p, arow[0:1, 88:89], arow[0:1, 88:89], -0.125, ALU.mult, reads=["csr"],
       writes=["csr"], tag=f"{tag}.hm4")
    tt(p, arow[0:1, 72:80], arow[0:1, 1:9], arow[0:1, 88:89].broadcast_to([1, 8]),
       ALU.add, reads=["abrow", "csr"], writes=["csr"], tag=f"{tag}.hc")
    act(p, arow[0:1, 72:80], arow[0:1, 72:80], AF.Tanh, reads=["csr"],
        writes=["csr"], tag=f"{tag}.th")
    tscalar2(p, arow[0:1, 72:80], arow[0:1, 72:80], 0.5, 1.0, ALU.mult, ALU.add,
             reads=["csr"], writes=["csr"], tag=f"{tag}.t5")
    tt(p, arow[0:1, 72:80], arow[0:1, 72:80], arow[0:1, 96:97].broadcast_to([1, 8]),
       ALU.mult, reads=["csr", "gsc"], writes=["csr"], tag=f"{tag}.cs")
    tt(p, arow[0:1, 80:88], arow[0:1, 9:17], arow[0:1, 72:80], ALU.mult,
       reads=["abrow", "csr"], writes=["gcs"], tag=f"{tag}.gcs")
    tscalar2(p, arow[0:1, 64:72], arow[0:1, 72:80], 1.0, -1.0, ALU.add, ALU.mult,
             reads=["csr"], writes=["ncl"], tag=f"{tag}.ncl")
    matmul(p, psO[:, 0:17], c["onesf32"][0:1, :], arow[0:1, 80:97], start=True,
           stop=True, reads=["gcs", "ncl", "gsc", "onesf32"], writes=[("psO", 0)],
           tag=f"{tag}.bc1")
    matmul(p, psO[:, 32:40], c["onesf32"][0:1, :], arow[0:1, 64:72], start=True,
           stop=True, reads=["ncl", "onesf32"], writes=[("psO", 0)],
           tag=f"{tag}.bc2")
    matmul(p, psO[:, 48:49], c["onesf32"][0:1, :], arow[0:1, 230:231], start=True,
           stop=True, reads=["lamt", "onesf32"], writes=[("psO", 0)],
           tag=f"{tag}.bc3")
    vcopy(p, c["ab_sc"][:, 0:8], psO[:, 0:8], reads=[("psO", 0)], writes=["absc"],
          tag=f"{tag}.bce1")
    vcopy(p, c["ab_bi"][:, :], psO[:, 32:40], reads=[("psO", 0)], writes=["abbi"],
          tag=f"{tag}.bce2")
    vcopy(p, c["ab_sc"][:, 8:9], psO[:, 48:49], reads=[("psO", 0)], writes=["absc"],
          tag=f"{tag}.bce3")
    ts(p, c["ab_sc"][:, 9:10], c["ab_sc"][:, 9:10], 0.5, ALU.mult, reads=["absc"],
       writes=["absc"], tag=f"{tag}.sw5")
    matmul(p, psO[:, 64:72], c["onesf32"][0:1, :], arow[0:1, 72:80], start=True,
           stop=True, reads=["csr", "onesf32"], writes=[("psO", 0)],
           tag=f"{tag}.bc4")
    vcopy(p, c["ab_sc"][:, 10:18], psO[:, 64:72], reads=[("psO", 0)],
          writes=["absc"], tag=f"{tag}.bce4")

    def projraw(which, wkey, dst, dkeyf, oc, bb, norm_accum):
        kw = (f"{tag}.w", bb)
        dma(p, "gpsimd", c["ab_w"][:, bb, :, :], W[wkey][oc], reads=(),
            writes=[kw], tag=f"{tag}.l{which}.{oc}")
        for nb in range(NBLK):
            sl = slice(nb * 512, (nb + 1) * 512)
            for kc in range(NCH):
                matmul(p, psS[:, sl], c["ab_w"][:, bb, kc, :], xn[:, kc, sl],
                       start=(kc == 0), stop=(kc == 7),
                       reads=[kw, ("xn", kc)], writes=[("psS", nb)],
                       tag=f"{tag}.m{which}.{oc}.{nb}.{kc}")
        act(p, dst, psS[:, :], AF.Identity, reads=kS, writes=[dkeyf],
            tag=f"{tag}.e{which}.{oc}")
        if norm_accum:
            act(p, lnsq[:, :], dst, AF.Square, reads=[dkeyf], writes=["lnsq"],
                tag=f"{tag}.sq{which}{oc}")
            for nb in range(NBLK):
                sl = slice(nb * 512, (nb + 1) * 512)
                matmul(p, psO[0:16, sl], c["ab_np"][:, oc, :], lnsq[:, sl],
                       start=(oc == 0), stop=(oc == 7),
                       reads=["lnsq", "npat2"], writes=[("psO", nb)],
                       tag=f"{tag}.mn{which}.{oc}.{nb}")

    def normalize(src, skey, tag2):
        act(p, lntmp[0:16, :], psO[0:16, :], AF.Sqrt, reads=kO, writes=kLT,
            tag=f"{tag2}.sqrt")
        ts(p, lntmp[0:16, :], lntmp[0:16, :], 1e-6, ALU.max, reads=kLT,
           writes=kLT, tag=f"{tag2}.eps")
        recip(p, lntmp[0:16, :], lntmp[0:16, :], reads=kLT, writes=kLT,
              tag=f"{tag2}.rec")
        for oc in range(8):
            for nb in range(NBLK):
                sl = slice(nb * 512, (nb + 1) * 512)
                bb = nb % 2
                matmul(p, psO[:, 1024 + bb * 512:1536 + bb * 512],
                       c["ab_sel"][:, oc, :], lntmp[0:16, sl], start=True,
                       stop=True, reads=["selp"] + kLT, writes=[("psO", 2 + bb)],
                       tag=f"{tag2}.bm.{oc}.{nb}")
                act(p, c["ab_rnb"][:, bb, :], psO[:, 1024 + bb * 512:1536 + bb * 512],
                    AF.Identity, reads=[("psO", 2 + bb)], writes=[("rnb", bb)],
                    tag=f"{tag2}.be.{oc}.{nb}")
                tt(p, src[:, oc, sl], src[:, oc, sl], c["ab_rnb"][:, bb, :],
                   ALU.mult, reads=[skey, ("rnb", bb)], writes=[skey],
                   tag=f"{tag2}.nm.{oc}.{nb}")

    for oc in range(8):
        projraw("q", "wq", qn[:, oc, :], "abqall", oc, oc % 2, True)
    normalize(qn, "abqall", f"{tag}.qn")
    for oc in range(8):
        projraw("k", "wk", kr[:, oc, :], "abkall", oc, oc % 2, True)
    normalize(kr, "abkall", f"{tag}.kn")
    for oc in range(8):
        dma(p, "gpsimd", c["ab_ksrc"][oc], kr[:, oc, :], reads=["abkall"],
            writes=["ksrc2"], tag=f"{tag}.sk.{oc}")
    for oc in range(8):
        bb = oc % 2
        kw = (f"{tag}.w", bb)
        dma(p, "gpsimd", c["ab_w"][:, bb, :, :], W["wv"][oc], reads=(),
            writes=[kw], tag=f"{tag}.lv.{oc}")
        for nb in range(NBLK):
            sl = slice(nb * 512, (nb + 1) * 512)
            for kc in range(NCH):
                matmul(p, psS[:, sl], c["ab_w"][:, bb, kc, :], xn[:, kc, sl],
                       start=(kc == 0), stop=(kc == 7),
                       reads=[kw, ("xn", kc)], writes=[("psS", nb)],
                       tag=f"{tag}.mv.{oc}.{nb}.{kc}")
        act(p, vch[:, bb, :], psS[:, :], AF.Identity, reads=kS,
            writes=[("abvc", bb)], tag=f"{tag}.ev.{oc}")
        dma(p, "gpsimd", c["ab_vsrc"][oc], vch[:, bb, :], reads=[("abvc", bb)],
            writes=["vsrc2"], tag=f"{tag}.sv.{oc}")
    collective(p, "AllGather", ALU.bypass, grp, ins=[c["ab_ksrc"]],
               outs=[c["ab_kg"]], reads=["ksrc2"], writes=["kg2"], tag=f"{tag}.gk")
    collective(p, "AllGather", ALU.bypass, grp, ins=[c["ab_vsrc"]],
               outs=[c["ab_vg"]], reads=["vsrc2"], writes=["vg2"], tag=f"{tag}.gv")

    TR = int(os.environ.get("MK_ABTRUNC", "9"))
    Kn = kr[:, 0:4, :]
    KnD2 = kr[:, 4:8, :]
    memset(p, "gpsimd", c["xn"][:, :, :], 0.0,
           writes=[("aby", i) for i in range(8)] + [("xn", i) for i in range(8)]) if TR < 9 else None
    for h in range(8 if TR >= 9 else TR):
        kKn, kKD = ("Knb", h % 1), ("KnD", h % 1)
        dma(p, "sync", Kn[:, :, :], c["ab_kg"][:, h, :, :].rearrange("r p t -> p r t"),
            reads=["kg2"] + (["abkall"] if h == 0 else []), writes=[kKn],
            tag=f"{tag}.stk.{h}")
        ts(p, KnD2[64:128, :, :], Kn[64:128, :, :], -1.0, ALU.mult,
           reads=[kKn] + (["abkall"] if h == 0 else []), writes=[kKD],
           tag=f"{tag}.neg.{h}")
        vcopy(p, KnD2[0:64, :, :], Kn[0:64, :, :], reads=[kKn], writes=[kKD],
              tag=f"{tag}.cpk.{h}")
        dma(p, "sync", Vf[:, :, :], c["ab_vg"][:, h, :, :].rearrange("r p t -> p r t"),
            reads=["vg2"], writes=["Vfb2"], tag=f"{tag}.stv.{h}")
        memset(p, "gpsimd", Vt[:, :, :, 64:65], 1.0, writes=["Vtb2"])
        PART = os.environ.get("MK_ABPART", "full")
        if PART == "stage":
            continue
        for g in range(16):
            for j in range(4):
                kt = g * 4 + j
                src = Vf[:, kt // 16, (kt % 16) * 128:(kt % 16) * 128 + 128]
                dstp = psO[:, 1024 + j * 64:1088 + j * 64].bitcast(BF16)
                transpose(p, dstp, src, c["ident"][:, :],
                          reads=["Vfb2", "ident"], writes=[("psO", 2)],
                          tag=f"{tag}.vt.{h}.{kt}")
            vcopy(p, Vt[:, 4 * g:4 * g + 4, :, 0:64],
                  psO[:, 1024:1280].bitcast(BF16).rearrange("p (a s b) -> p a s b", a=4, s=2),
                  reads=[("psO", 2)], writes=["Vtb2"], tag=f"{tag}.vte.{h}.{g}")
        if PART == "vt":
            continue
        for qb in range(4):
            qsl = slice(qb * 512, (qb + 1) * 512)
            for kt in range(64):
                sb1 = kt % 2
                lh1 = Kn[0:64, kt // 16, (kt % 16) * 128:(kt % 16) * 128 + 128]
                lh2 = Kn[64:128, kt // 16, (kt % 16) * 128:(kt % 16) * 128 + 128]
                kL1 = ("psS", sb1)
                lhD = KnD2[:, kt // 16, (kt % 16) * 128:(kt % 16) * 128 + 128]
                matmul(p, psS[:, sb1 * 512:(sb1 + 1) * 512], lh1,
                       qn[0:64, h, qsl], start=True, stop=True,
                       reads=[kKn, "abqall"], writes=[kL1],
                       tag=f"{tag}.L1.{h}.{qb}.{kt}")
                matmul(p, psS[:, 1024:1536], lh2, qn[64:128, h, qsl],
                       start=True, stop=True, reads=[kKn, "abqall"],
                       writes=[("psS", 2)], tag=f"{tag}.L2.{h}.{qb}.{kt}")
                matmul(p, psS[:, 1536:2048], lhD, qn[:, h, qsl],
                       start=True, stop=True, reads=[kKD, "abqall"],
                       writes=[("psS", 3)], tag=f"{tag}.D.{h}.{qb}.{kt}")
                KTL = int(os.environ.get("MK_ABKT", "5"))
                if KTL < 2:
                    continue
                act(p, c["ab_g"][:, :], psS[:, 1536:2048], AF.Tanh,
                    scale=c["ab_sc"][:, h:h + 1], reads=[("psS", 3), "absc"],
                    writes=["abg"], tag=f"{tag}.G.{h}.{qb}.{kt}")
                if KTL < 3:
                    continue
                stt(p, ein[:, 0, :], psS[:, sb1 * 512:(sb1 + 1) * 512],
                    c["ab_sc"][:, 10 + h:11 + h], c["ab_g"][:, :], ALU.mult,
                    ALU.add, reads=[kL1, "abg", "absc"],
                    writes=["ein1"], tag=f"{tag}.T1.{h}.{qb}.{kt}")
                stt(p, ein[:, 1, :], psS[:, 1024:1536],
                    c["ab_sc"][:, 10 + h:11 + h], c["ab_g"][:, :], ALU.mult,
                    ALU.subtract, reads=[("psS", 2), "abg", "absc"],
                    writes=["ein2"], tag=f"{tag}.T2.{h}.{qb}.{kt}")
                if KTL < 4:
                    continue
                act(p, eb[:, sb1, 0, :], ein[:, 0, :], AF.Exp,
                    bias=c["ab_bi"][:, h:h + 1], reads=["ein1", "abbi"],
                    writes=[("eb1", sb1)], tag=f"{tag}.e1.{h}.{qb}.{kt}")
                act(p, eb[:, sb1, 1, :], ein[:, 1, :], AF.Exp,
                    bias=c["ab_bi"][:, h:h + 1], reads=["ein2", "abbi"],
                    writes=[("eb2", sb1)], tag=f"{tag}.e2.{h}.{qb}.{kt}")
                if KTL < 5:
                    continue
                for dvh in range(2):
                    matmul(p, psO[0:65, dvh * 512:(dvh + 1) * 512],
                           Vt[:, kt, dvh, :], eb[:, sb1, 0, :],
                           start=(kt == 0), stop=(kt == 63),
                           reads=["Vtb2", ("eb1", sb1)], writes=[("psO", dvh)],
                           tag=f"{tag}.p1.{h}.{qb}.{kt}.{dvh}")
                    matmul(p, psO[0:65, 1024 + dvh * 512:1024 + (dvh + 1) * 512],
                           Vt[:, kt, dvh, :], eb[:, sb1, 1, :],
                           start=(kt == 0), stop=(kt == 63),
                           reads=["Vtb2", ("eb2", sb1)], writes=[("psO", 2 + dvh)],
                           tag=f"{tag}.p2.{h}.{qb}.{kt}.{dvh}")
            if PART == "kt":
                vcopy(p, c["ab_yc"][:, :], psO[0:64, 0:512].broadcast_to([128, 512]) if False else c["ab_yc"][:, :], reads=[("psO", 0), ("psO", 1), ("psO", 2), ("psO", 3)], writes=["abyc"], tag=f"{tag}.drain.{h}.{qb}")
                continue
            recip(p, rows[64:65, 0:512], psO[64:65, 0:512], reads=[("psO", 0)],
                  writes=["rz1"], tag=f"{tag}.rz1.{h}.{qb}")
            recip(p, rows[64:65, 512:1024], psO[64:65, 1024:1536],
                  reads=[("psO", 2)], writes=["rz2"], tag=f"{tag}.rz2.{h}.{qb}")
            matmul(p, psS[0:128, 0:512], c["ones65"][64:65, 0:128],
                   rows[64:65, 0:512], start=True, stop=True,
                   reads=["rz1", "aao1"], writes=[("psS", 0)],
                   tag=f"{tag}.rb1.{h}.{qb}")
            matmul(p, psS[0:128, 512:1024], c["ones65"][64:65, 0:128],
                   rows[64:65, 512:1024], start=True, stop=True,
                   reads=["rz2", "aao1"], writes=[("psS", 1)],
                   tag=f"{tag}.rb2.{h}.{qb}")
            act(p, c["ab_rb"][:, 0, :], psS[:, 0:512], AF.Identity,
                reads=[("psS", 0)], writes=["rb1s"], tag=f"{tag}.rbe1.{h}.{qb}")
            act(p, c["ab_rb"][:, 1, :], psS[:, 512:1024], AF.Identity,
                scale=c["ab_sc"][:, 8:9], reads=[("psS", 1), "absc"],
                writes=["rb2s"], tag=f"{tag}.rbe2.{h}.{qb}")
            for dvh in range(2):
                rsl = slice(dvh * 64, (dvh + 1) * 64)
                tt(p, c["ab_yc"][rsl, :], psO[0:64, dvh * 512:(dvh + 1) * 512],
                   c["ab_rb"][rsl, 0, :], ALU.mult,
                   reads=[("psO", dvh), "rb1s"], writes=["abyc"],
                   tag=f"{tag}.y1.{h}.{qb}.{dvh}")
                tt(p, lntmp[rsl, 0:512], psO[0:64, 1024 + dvh * 512:1024 + (dvh + 1) * 512],
                   c["ab_rb"][rsl, 1, :], ALU.mult,
                   reads=[("psO", 2 + dvh), "rb2s", "lntmp"], writes=["lntmp"],
                   tag=f"{tag}.y2.{h}.{qb}.{dvh}")
                tt(p, c["ab_yc"][rsl, :], c["ab_yc"][rsl, :], lntmp[rsl, 0:512],
                   ALU.add, reads=["abyc", "lntmp"], writes=["abyc"],
                   tag=f"{tag}.y3.{h}.{qb}.{dvh}")
            act(p, lnsq[:, 0:512], c["ab_yc"][:, :], AF.Square, reads=["abyc"],
                writes=["lnsq"], tag=f"{tag}.ysq.{h}.{qb}")
            matmul(p, psS[0:1, 1024:1536], c["onescol"][:, 0:1], lnsq[:, 0:512],
                   start=True, stop=True, reads=["lnsq", "onescol"],
                   writes=[("psS", 2)], tag=f"{tag}.yss.{h}.{qb}")
            epsap, epsk = cconst(p, c, 1e-5)
            act(p, rows[0:1, 1024:1536], psS[0:1, 1024:1536], AF.Sqrt,
                scale=1.0 / 128.0, bias=epsap[0:1, :],
                reads=[("psS", 2), epsk], writes=["yrs"], tag=f"{tag}.ysr.{h}.{qb}")
            recip(p, rows[0:1, 1024:1536], rows[0:1, 1024:1536], reads=["yrs"],
                  writes=["yrs"], tag=f"{tag}.yrc.{h}.{qb}")
            matmul(p, psS[:, 1536:2048], c["onesf32"][0:1, :], rows[0:1, 1024:1536],
                   start=True, stop=True, reads=["yrs", "onesf32"],
                   writes=[("psS", 3)], tag=f"{tag}.yrb.{h}.{qb}")
            stt(p, y[:, h, qsl], c["ab_yc"][:, :], c["ab_sc"][:, 9:10],
                psS[:, 1536:2048], ALU.mult, ALU.mult,
                reads=["abyc", "absc", ("psS", 3)],
                writes=[("aby", h), ("xn", h)], tag=f"{tag}.yo.{h}.{qb}")
    xin_r = c["xin_dram"].rearrange("(c p) t -> p c t", p=128)
    xout_r = c["xout_dram"].rearrange("(c p) t -> p c t", p=128)
    if "ab_bob" not in c:
        c["ab_bob"] = sb("ab_bob", [1, 1024], BF16)
    dma(p, "gpsimd", c["ab_bob"][:, :], W["bo"], reads=(), writes=["abbo"],
        tag=f"{tag}.lbo")
    for oc in range(8):
        bb = oc % 2
        kw = (f"{tag}.w", bb)
        dma(p, "gpsimd", c["ab_w"][:, bb, :, :], W["wo"][oc], reads=(),
            writes=[kw], tag=f"{tag}.lo.{oc}")
        for nb in range(NBLK):
            sl = slice(nb * 512, (nb + 1) * 512)
            for kc in range(NCH):
                matmul(p, psS[:, sl], c["ab_w"][:, bb, kc, :], y[:, kc, sl],
                       start=(kc == 0), stop=False,
                       reads=[kw, ("aby", kc)], writes=[("psS", nb)],
                       tag=f"{tag}.mo.{oc}.{nb}.{kc}")
            matmul(p, psS[:, sl], c["ab_bob"][0:1, oc * 128:(oc + 1) * 128],
                   c["onesrow"][0:1, 0:512], start=False, stop=True,
                   reads=["abbo", "onesrow"], writes=[("psS", nb)],
                   tag=f"{tag}.mbo.{oc}.{nb}")
            dma(p, "sync", c["xs"][:, nb % 2, 0:512], xin_r[:, oc, sl], reads=[],
                writes=[("xs", nb % 2)], tag=f"{tag}.rlx.{oc}.{nb}")
            tt(p, c["xob"][:, nb % 2, :], psS[:, sl], c["xs"][:, nb % 2, 0:512],
               ALU.add, reads=[("psS", nb), ("xs", nb % 2)],
               writes=[("xob", nb % 2)], tag=f"{tag}.res.{oc}.{nb}")
            dma(p, "sync", xout_r[:, oc, sl], c["xob"][:, nb % 2, :],
                reads=[("xob", nb % 2)], writes=[], tag=f"{tag}.rsx.{oc}.{nb}")

def prep_l0ab(inputs):
    W = {}
    W["wq"] = _swz(np.ascontiguousarray(inputs["L0_attn_b_Wq"].T))
    W["wk"] = _swz(np.ascontiguousarray(inputs["L0_attn_b_Wk"].T))
    W["wv"] = _swz(np.ascontiguousarray(inputs["L0_attn_b_Wv"].T))
    W["wo"] = _swz(np.ascontiguousarray(inputs["L0_attn_b_Wo"].T))
    W["bo"] = np.ascontiguousarray(inputs["L0_attn_b_bo"][None].astype(np.float32))
    npat = np.zeros((128, 8, 16), np.float32)
    for oc in range(8):
        npat[0:64, oc, 2 * oc] = 1.0
        npat[64:128, oc, 2 * oc + 1] = 1.0
    W["npat"] = npat
    selp = np.zeros((16, 8, 128), np.float32)
    for oc in range(8):
        selp[2 * oc, oc, 0:64] = 1.0
        selp[2 * oc + 1, oc, 64:128] = 1.0
    W["selpat"] = selp
    smalls = np.zeros((1, 80), np.float32)
    smalls[0, 0] = float(inputs["L0_attn_b_cos_logit_scale_raw"])
    smalls[0, 1:9] = inputs["L0_attn_b_cos_head_delta"]
    smalls[0, 9:17] = inputs["L0_attn_b_delta_gain"]
    W["smalls"] = smalls
    lam = np.concatenate([inputs["L0_attn_b_lambda_q1"], inputs["L0_attn_b_lambda_k1"],
                          inputs["L0_attn_b_lambda_q2"], inputs["L0_attn_b_lambda_k2"]])
    W["lam"] = np.ascontiguousarray(lam[None].astype(np.float32))
    W["swcol"] = np.ascontiguousarray(inputs["L0_attn_b_subln_weight"][:, None].astype(np.float32))
    return W

AB_W_SHAPES = {
    "wq": ([8, 128, 8, 128], F32), "wk": ([8, 128, 8, 128], F32),
    "wv": ([8, 128, 8, 128], F32), "wo": ([8, 128, 8, 128], F32),
    "bo": ([1, 1024], F32), "npat": ([128, 8, 16], F32),
    "selpat": ([16, 8, 128], F32), "smalls": ([1, 80], F32),
    "lam": ([1, 256], F32), "swcol": ([128, 1], F32),
}


def emit_l1aa(p, c, es, W, tag):
    nc = p.nc
    sb = lambda name, shape, dt: es.enter_context(nc.sbuf_tensor(name, shape, dt))
    dr = lambda name, shape, dt: nc.dram_tensor(name, shape, dt).ap()
    if "la_q" not in c:
        c["la_q"] = sb("la_q", [128, 8, 2048], BF16)
        c["la_kch"] = sb("la_kch", [128, 2, 2048], BF16)
        c["la_vch"] = sb("la_vch", [128, 2, 2048], BF16)
        c["la_kst"] = sb("la_kst", [128, 2, 2048], BF16)
        c["la_vst"] = sb("la_vst", [128, 2, 2048], BF16)
        c["la_t"] = sb("la_t", [128, 2, 2048], BF16)
        c["la_r"] = sb("la_r", [16, 2048], F32)
        c["la_c"] = sb("la_c", [16, 2048], BF16)
        c["la_sel"] = sb("la_sel", [16, 8, 128], BF16)
        c["la_np"] = sb("la_np", [128, 8, 16], BF16)
        c["la_w"] = sb("la_w", [16, 2], F32)
        c["la_wb"] = sb("la_wb", [128, 2, 8, 128], BF16)
        c["la_src"] = dr("la_src", [2, 1024, 2048], BF16)
        c["la_g"] = dr("la_g", [2, 2, 1024, 2048], BF16)
    q, kch, vch = c["la_q"], c["la_kch"], c["la_vch"]
    kst, vst, tb = c["la_kst"], c["la_vst"], c["la_t"]
    xn, psS, psO = c["xn"], c["psS"], c["psO"]
    y = xn
    rows, lntmp, lnsq = c["rowbuf"], c["lntmp"], c["lnsq"]
    kS = [("psS", i) for i in range(4)]
    kO = [("psO", i) for i in range(4)]
    kLT = ["lntmp", "r3", "r2", "r0sq"]
    grp = [[0, 4], [1, 5], [2, 6], [3, 7]]

    dma(p, "gpsimd", c["la_np"][:, :, :], W["npat"], reads=(), writes=["lnp"], tag=f"{tag}.lnp")
    dma(p, "gpsimd", c["la_sel"][:, :, :], W["selpat"], reads=(), writes=["lsel"], tag=f"{tag}.lsel")
    dma(p, "sync", c["la_w"][:, :], W["wrow"], reads=(), writes=["lw"], tag=f"{tag}.lw")

    def proj(which, wkey, oc, bb, dst, dkey, relu):
        kw = (f"{tag}.w", bb)
        dma(p, "gpsimd", c["la_wb"][:, bb, :, :], W[wkey][oc], reads=(),
            writes=[kw], tag=f"{tag}.l{which}.{oc}")
        for nb in range(NBLK):
            sl = slice(nb * 512, (nb + 1) * 512)
            for kc in range(NCH):
                matmul(p, psS[:, sl], c["la_wb"][:, bb, kc, :], xn[:, kc, sl],
                       start=(kc == 0), stop=(kc == 7),
                       reads=[kw, ("xn", kc)], writes=[("psS", nb)],
                       tag=f"{tag}.m{which}.{oc}.{nb}.{kc}")
        act(p, dst, psS[:, :], AF.Relu if relu else AF.Identity, reads=kS,
            writes=[dkey], tag=f"{tag}.e{which}.{oc}")

    for oc in range(8):
        proj("q", "wq", oc, oc % 2, q[:, oc, :], "laq", True)
    for oc in range(8):
        bb = oc % 2
        proj("k", "wk", oc, bb, kch[:, bb, :], ("lakc", bb), True)
        dma(p, "gpsimd", c["la_src"][0, oc * 128:(oc + 1) * 128, :], kch[:, bb, :],
            reads=[("lakc", bb)], writes=["lasrc"], tag=f"{tag}.sk.{oc}")
    for oc in range(8):
        bb = oc % 2
        proj("v", "wv", oc, bb, vch[:, bb, :], ("lavc", bb), False)
        dma(p, "gpsimd", c["la_src"][1, oc * 128:(oc + 1) * 128, :], vch[:, bb, :],
            reads=[("lavc", bb)], writes=["lasrc"], tag=f"{tag}.sv.{oc}")
    collective(p, "AllGather", ALU.bypass, grp, ins=[c["la_src"]],
               outs=[c["la_g"]], reads=["lasrc"], writes=["lag"], tag=f"{tag}.g")
    for r in range(2):
        ps, kp = (psS, "psS") if r == 0 else (psO, "psO")
        for oc in range(8):
            bb = oc % 2
            dma(p, "sync", kst[:, bb, :], c["la_g"][r, 0, oc * 128:(oc + 1) * 128, :],
                reads=["lag"], writes=[("lakst", bb)], tag=f"{tag}.ldk.{r}.{oc}")
            tt(p, lnsq[:, :], q[:, oc, :], kst[:, bb, :], ALU.mult,
               reads=["laq", ("lakst", bb)], writes=["lnsq"],
               tag=f"{tag}.qk.{r}.{oc}")
            for nb in range(NBLK):
                sl = slice(nb * 512, (nb + 1) * 512)
                matmul(p, ps[0:16, sl], c["la_np"][:, oc, :], lnsq[:, sl],
                       start=(oc == 0), stop=(oc == 7),
                       reads=["lnsq", "lnp"], writes=[(kp, nb)],
                       tag=f"{tag}.ms.{r}.{oc}.{nb}")
    A, B, C = lntmp[0:16, :], rows[0:16, 0:2048], c["la_r"][:, :]
    ts(p, A, psS[0:16, :], c["la_w"][:, 0:1], ALU.mult, reads=kS + ["lw"],
       writes=kLT, tag=f"{tag}.sA")
    ts(p, B, psO[0:16, :], c["la_w"][:, 1:2], ALU.mult, reads=kO + ["lw"],
       writes=["r0", "qn2", "kn2loc", "kn2s"], tag=f"{tag}.sB")
    tt(p, C, A, B, ALU.add, reads=kLT + ["r0"], writes=["lar"], tag=f"{tag}.dn")
    ts(p, C, C, 1e-6, ALU.max, reads=["lar"], writes=["lar"], tag=f"{tag}.mx")
    recip(p, C, C, reads=["lar"], writes=["lar"], tag=f"{tag}.rc")
    tt(p, A, A, C, ALU.mult, reads=kLT + ["lar"], writes=kLT, tag=f"{tag}.c0")
    tt(p, B, B, C, ALU.mult, reads=["r0", "lar"], writes=["r0"], tag=f"{tag}.c1")
    vcopy(p, lnsq[0:16, :], A, reads=kLT, writes=["lnsq"], tag=f"{tag}.c0b")
    vcopy(p, c["la_c"][:, :], B, reads=["r0"], writes=["lac"], tag=f"{tag}.c1b")
    for oc in range(8):
        bb = oc % 2
        dma(p, "sync", vst[:, 0, :], c["la_g"][0, 1, oc * 128:(oc + 1) * 128, :],
            reads=["lag"], writes=[("lavst", 0)], tag=f"{tag}.ldv0.{oc}")
        dma(p, "sync", vst[:, 1, :], c["la_g"][1, 1, oc * 128:(oc + 1) * 128, :],
            reads=["lag"], writes=[("lavst", 1)], tag=f"{tag}.ldv1.{oc}")
        for nb in range(NBLK):
            sl = slice(nb * 512, (nb + 1) * 512)
            pb = nb % 2
            matmul(p, psS[:, pb * 512:(pb + 1) * 512], c["la_sel"][:, oc, :],
                   lnsq[0:16, sl], start=True, stop=True,
                   reads=["lsel", "lnsq"], writes=[("psS", pb)],
                   tag=f"{tag}.b0.{oc}.{nb}")
            matmul(p, psO[:, pb * 512:(pb + 1) * 512], c["la_sel"][:, oc, :],
                   c["la_c"][:, sl], start=True, stop=True,
                   reads=["lsel", "lac"], writes=[("psO", pb)],
                   tag=f"{tag}.b1.{oc}.{nb}")
            tt(p, tb[:, 0, sl], vst[:, 0, sl], psS[:, pb * 512:(pb + 1) * 512],
               ALU.mult, reads=[("lavst", 0), ("psS", pb)], writes=[("lat", 0)],
               tag=f"{tag}.t0.{oc}.{nb}")
            tt(p, tb[:, 1, sl], vst[:, 1, sl], psO[:, pb * 512:(pb + 1) * 512],
               ALU.mult, reads=[("lavst", 1), ("psO", pb)], writes=[("lat", 1)],
               tag=f"{tag}.t1.{oc}.{nb}")
            tt(p, y[:, oc, sl], tb[:, 0, sl], tb[:, 1, sl], ALU.add,
               reads=[("lat", 0), ("lat", 1)], writes=[("lay", oc), ("xn", oc)],
               tag=f"{tag}.y.{oc}.{nb}")
    xin_r = c["xin_dram"].rearrange("(c p) t -> p c t", p=128)
    xout_r = c["xout_dram"].rearrange("(c p) t -> p c t", p=128)
    for oc in range(8):
        bb = oc % 2
        kw = (f"{tag}.w", bb)
        dma(p, "gpsimd", c["la_wb"][:, bb, :, :], W["wo"][oc], reads=(),
            writes=[kw], tag=f"{tag}.lo.{oc}")
        for nb in range(NBLK):
            sl = slice(nb * 512, (nb + 1) * 512)
            for kc in range(NCH):
                matmul(p, psS[:, sl], c["la_wb"][:, bb, kc, :], y[:, kc, sl],
                       start=(kc == 0), stop=(kc == 7),
                       reads=[kw, ("lay", kc)], writes=[("psS", nb)],
                       tag=f"{tag}.mo.{oc}.{nb}.{kc}")
            dma(p, "sync", c["xs"][:, nb % 2, 0:512], xin_r[:, oc, sl], reads=[],
                writes=[("xs", nb % 2)], tag=f"{tag}.rlx.{oc}.{nb}")
            tt(p, c["xob"][:, nb % 2, :], psS[:, sl], c["xs"][:, nb % 2, 0:512],
               ALU.add, reads=[("psS", nb), ("xs", nb % 2)],
               writes=[("xob", nb % 2)], tag=f"{tag}.res.{oc}.{nb}")
            dma(p, "sync", xout_r[:, oc, sl], c["xob"][:, nb % 2, :],
                reads=[("xob", nb % 2)], writes=[], tag=f"{tag}.rsx.{oc}.{nb}")

def prep_l1aa(inputs):
    W = {}
    W["wq"] = _swz(np.ascontiguousarray(inputs["L1_attn_a_Wq"].T))
    W["wk"] = _swz(np.ascontiguousarray(inputs["L1_attn_a_Wk"].T))
    W["wv"] = _swz(np.ascontiguousarray(inputs["L1_attn_a_Wv"].T))
    W["wo"] = _swz(np.ascontiguousarray(inputs["L1_attn_a_Wo"].T))
    npat = np.zeros((128, 8, 16), np.float32)
    for oc in range(8):
        npat[0:64, oc, 2 * oc] = 1.0
        npat[64:128, oc, 2 * oc + 1] = 1.0
    W["npat"] = npat
    selp = np.zeros((16, 8, 128), np.float32)
    for oc in range(8):
        selp[2 * oc, oc, 0:64] = 1.0
        selp[2 * oc + 1, oc, 64:128] = 1.0
    W["selpat"] = selp
    idx = (np.pi / 2 * np.arange(1, 3, dtype=np.float32) / 2).astype(np.float32)
    s, cz = np.sin(idx), np.cos(idx)
    W2 = (s[:, None] * s[None, :] + cz[:, None] * cz[None, :]).astype(np.float32)
    W["wrow_per_core"] = []
    for b in range(2):
        W["wrow_per_core"].append(
            np.ascontiguousarray(np.tile(W2[b][None, :], (16, 1)).astype(np.float32)))
    return W

LA_W_SHAPES = {
    "wq": ([8, 128, 8, 128], F32), "wk": ([8, 128, 8, 128], F32),
    "wv": ([8, 128, 8, 128], F32), "wo": ([8, 128, 8, 128], F32),
    "npat": ([128, 8, 16], F32), "selpat": ([16, 8, 128], F32),
    "wrow": ([16, 2], F32),
}


def emit_l1ab(p, c, es, W, tag):
    nc = p.nc
    sb = lambda name, shape, dt: es.enter_context(nc.sbuf_tensor(name, shape, dt))
    dr = lambda name, shape, dt: nc.dram_tensor(name, shape, dt).ap()
    if "mb_cb" not in c:
        c["mb_cb"] = sb("mb_cb", [128, 2051], BF16)
        c["mb_w"] = sb("mb_w", [128, 2, 8, 128], BF16)
        c["mb_wx"] = sb("mb_wx", [128, 8, 96], BF16)
        c["mb_wdt"] = sb("mb_wdt", [64, 16, 128], BF16)
        c["mb_xdb"] = sb("mb_xdb", [64, 2048], BF16)
        c["mb_B16"] = sb("mb_B16", [16, 2048], BF16)
        c["mb_C16"] = sb("mb_C16", [16, 2048], BF16)
        c["mb_dc"] = sb("mb_dc", [128, 2, 2048], BF16)
        c["mb_uc"] = sb("mb_uc", [128, 2, 2048], BF16)
        c["mb_du"] = sb("mb_du", [128, 2048], BF16)
        c["mb_sc2"] = sb("mb_sc2", [128, 2, 2048], BF16)
        c["mb_sr"] = sb("mb_sr", [128, 2, 2048], BF16)
        c["mb_da"] = sb("mb_da", [128, 2048], BF16)
        c["mb_dbu"] = sb("mb_dbu", [128, 2048], BF16)
        c["mb_h"] = sb("mb_h", [128, 2048], F32)
        c["mb_hc"] = sb("mb_hc", [128, 2048], BF16)
        c["mb_br"] = sb("mb_br", [128, 2048], BF16)
        c["mb_cr"] = sb("mb_cr", [128, 2048], BF16)
        c["mb_A"] = sb("mb_A", [128, 256], F32)
        c["mb_hf"] = sb("mb_hf", [128, 2, 256], F32)
        c["mb_hin"] = sb("mb_hin", [128, 256], F32)
        c["mb_ch"] = sb("mb_ch", [128, 4, 512], F32)
        c["mb_sf"] = sb("mb_sf", [128, 16], BF16)
        c["mb_sel"] = sb("mb_sel", [128, 16, 128], BF16)
        c["mb_red"] = sb("mb_red", [128, 16, 128], BF16)
        c["mb_p16"] = sb("mb_p16", [16, 128], BF16)
        c["mb_kc"] = sb("mb_kc", [128, 16, 4], F32)
        c["mb_bc"] = sb("mb_bc", [128, 16], F32)
        c["mb_D"] = sb("mb_D", [128, 16], F32)
        c["mb_rs"] = sb("mb_rs", [128, 4], F32)
        c["mb_hx"] = sb("mb_hx", [128, 8, 3], BF16)
        c["mb_bdt"] = sb("mb_bdt", [1, 2048], BF16)
        c["mb_bout"] = sb("mb_bout", [1, 1024], BF16)
        c["mb_o1"] = sb("mb_o1", [128, 2048], BF16)
        c["mb_ud"] = dr("mb_ud", [16, 128, 2048], BF16)
        c["mb_srd"] = dr("mb_srd", [16, 128, 2048], BF16)
        c["mb_sd"] = dr("mb_sd", [16, 128, 2048], BF16)
        c["mb_dd"] = dr("mb_dd", [16, 128, 2048], BF16)
        c["mb_y1"] = dr("mb_y1", [16, 128, 2048], BF16)
        c["mb_yo"] = dr("mb_yo", [16, 128, 2048], BF16)
        c["mb_hsrc"] = dr("mb_hsrc", [128, 512], F32)
        c["mb_hg"] = dr("mb_hg", [4, 128, 512], F32)
    xn, psS, psO = c["xn"], c["psS"], c["psO"]
    rows, lntmp, lnsq = c["rowbuf"], c["lntmp"], c["lnsq"]
    kS = [("psS", i) for i in range(4)]
    kO = [("psO", i) for i in range(4)]
    kLT = ["lntmp", "r3", "r2", "r0sq"]
    grp = [[0, 1, 2, 3], [4, 5, 6, 7]]
    cb = c["mb_cb"]

    for nm, key in [("sel", "msel"), ("red", "mred")]:
        dma(p, "gpsimd", c["mb_" + nm][:, :, :], W[nm + "k"], reads=(),
            writes=[key], tag=f"{tag}.l{nm}")
    dma(p, "gpsimd", c["mb_p16"][:, :], W["pat16"], reads=(), writes=["mp16"], tag=f"{tag}.lp")
    dma(p, "gpsimd", c["mb_kc"][:, :, :], W["kconv"], reads=(), writes=["mkc"], tag=f"{tag}.lk")
    dma(p, "gpsimd", c["mb_bc"][:, :], W["bconv"], reads=(), writes=["mbc"], tag=f"{tag}.lbc")
    dma(p, "gpsimd", c["mb_D"][:, :], W["Dcol"], reads=(), writes=["mD"], tag=f"{tag}.lD")
    dma(p, "gpsimd", c["mb_rs"][:, :], W["rsel"], reads=(), writes=["mrs"], tag=f"{tag}.lrs")
    dma(p, "gpsimd", c["mb_A"][:, :], W["Acol"], reads=(), writes=["mA"], tag=f"{tag}.lA")
    dma(p, "gpsimd", c["mb_bdt"][:, :], W["bdt"], reads=(), writes=["mbdt"], tag=f"{tag}.lbdt")
    dma(p, "gpsimd", c["mb_bout"][:, :], W["bout"], reads=(), writes=["mbout"], tag=f"{tag}.lbo")
    dma(p, "gpsimd", c["mb_hx"][:, :, :], W["xhalo"], reads=(), writes=["mhx"], tag=f"{tag}.lhx")
    dma(p, "gpsimd", c["mb_wx"][:, :, :], W["wx"].rearrange("a p b -> p a b"),
        reads=(), writes=["mwx"], tag=f"{tag}.lwx")
    dma(p, "gpsimd", c["mb_wdt"][:, :, :], W["wdt"], reads=(), writes=["mwdt"], tag=f"{tag}.lwdt")
    memset(p, "gpsimd", c["mb_o1"][:, :], 1.0, writes=["mo1"])
    act(p, c["mb_A"][:, :], c["mb_A"][:, :], AF.Exp, reads=["mA"], writes=["mA"],
        tag=f"{tag}.Ae")
    ts(p, c["mb_A"][:, :], c["mb_A"][:, :], -1.0, ALU.mult, reads=["mA"],
       writes=["mA"], tag=f"{tag}.An")
    for ch in range(NCH):
        act(p, lnsq[:, 0:3], c["mb_hx"][:, ch, :], AF.Square, reads=["mhx"],
            writes=["lnsq"], tag=f"{tag}.hsq{ch}")
        matmul(p, psS[0:1, 0:3], c["onescol"][:, 0:1], c["mb_hx"][:, ch, :],
               start=(ch == 0), stop=(ch == NCH - 1), reads=["mhx", "onescol"],
               writes=[("psS", 0)], tag=f"{tag}.hs{ch}")
        matmul(p, psS[0:1, 512:515], c["onescol"][:, 0:1], lnsq[:, 0:3],
               start=(ch == 0), stop=(ch == NCH - 1), reads=["lnsq", "onescol"],
               writes=[("psS", 1)], tag=f"{tag}.hq{ch}")
    ts(p, rows[0:1, 0:3], psS[0:1, 0:3], -1.0 / D, ALU.mult, reads=[("psS", 0)],
       writes=["hmu"], tag=f"{tag}.hmu")
    tt(p, rows[0:1, 4:7], rows[0:1, 0:3], rows[0:1, 0:3], ALU.mult, reads=["hmu"],
       writes=["hmu"], tag=f"{tag}.hm2")
    stt(p, rows[0:1, 8:11], psS[0:1, 512:515], 1.0 / D, rows[0:1, 4:7],
        ALU.mult, ALU.subtract, reads=[("psS", 1), "hmu"], writes=["hvar"],
        tag=f"{tag}.hvar")
    epsap, epsk = cconst(p, c, 1e-5)
    act(p, rows[0:1, 8:11], rows[0:1, 8:11], AF.Sqrt, bias=epsap[0:1, :],
        reads=["hvar", epsk], writes=["hvar"], tag=f"{tag}.hsrt")
    recip(p, rows[0:1, 8:11], rows[0:1, 8:11], reads=["hvar"], writes=["hvar"],
          tag=f"{tag}.hrc")
    matmul(p, psS[:, 0:3], c["onesf32"][0:1, :], rows[0:1, 0:3], start=True,
           stop=True, reads=["hmu", "onesf32"], writes=[("psS", 0)],
           tag=f"{tag}.hbm")
    matmul(p, psS[:, 512:515], c["onesf32"][0:1, :], rows[0:1, 8:11], start=True,
           stop=True, reads=["hvar", "onesf32"], writes=[("psS", 1)],
           tag=f"{tag}.hbr")
    for ch in range(NCH):
        tt(p, lnsq[:, 0:3], c["mb_hx"][:, ch, :], psS[:, 0:3], ALU.add,
           reads=["mhx", ("psS", 0)], writes=["lnsq"], tag=f"{tag}.hn1{ch}")
        tt(p, c["mb_hx"][:, ch, :], lnsq[:, 0:3], psS[:, 512:515], ALU.mult,
           reads=["lnsq", ("psS", 1)], writes=["mhx"], tag=f"{tag}.hn2{ch}")
    for nm, cols, dstap, dkey in (("B", slice(64, 80), c["mb_B16"][:, :], "mB16"),
                                  ("C", slice(80, 96), c["mb_C16"][:, :], "mC16"),
                                  ("dl", slice(0, 64), c["mb_xdb"][:, :], "mxdb")):
        for nb in range(NBLK):
            sl = slice(nb * 512, (nb + 1) * 512)
            for kc in range(NCH):
                matmul(p, psO[0:(16 if nm != "dl" else 64), sl],
                       c["mb_wx"][:, kc, cols], xn[:, kc, sl],
                       start=(kc == 0), stop=(kc == 7),
                       reads=["mwx", ("xn", kc)], writes=[("psO", nb)],
                       tag=f"{tag}.x{nm}.{nb}.{kc}")
        vcopy(p, dstap, psO[0:(16 if nm != "dl" else 64), :], reads=kO,
              writes=[dkey], tag=f"{tag}.xe{nm}")
    for nm, src, srck, dst, dk in (("br", c["mb_B16"], "mB16", c["mb_br"], "mbr"),
                                   ("cr", c["mb_C16"], "mC16", c["mb_cr"], "mcr")):
        for nb in range(NBLK):
            sl = slice(nb * 512, (nb + 1) * 512)
            matmul(p, psO[:, sl], c["mb_p16"][:, :], src[:, sl], start=True,
                   stop=True, reads=["mp16", srck], writes=[("psO", nb)],
                   tag=f"{tag}.{nm}m{nb}")
        act(p, dst[:, :], psO[:, :], AF.Identity, reads=kO, writes=[dk],
            tag=f"{tag}.{nm}e")
    for oc in range(16):
        bb = oc % 2
        for nb in range(NBLK):
            sl = slice(nb * 512, (nb + 1) * 512)
            matmul(p, psS[:, sl], c["mb_wdt"][:, oc, :], c["mb_xdb"][:, sl],
                   start=True, stop=False, reads=["mwdt", "mxdb"],
                   writes=[("psS", nb)], tag=f"{tag}.dm.{oc}.{nb}")
            matmul(p, psS[:, sl], c["mb_bdt"][0:1, oc * 128:(oc + 1) * 128],
                   c["onesrow"][0:1, 0:512], start=False, stop=True,
                   reads=["mbdt", "onesrow"], writes=[("psS", nb)],
                   tag=f"{tag}.db.{oc}.{nb}")
        act(p, lntmp[:, :], psS[:, :], AF.Abs, reads=kS, writes=kLT,
            tag=f"{tag}.sp1.{oc}")
        act(p, lnsq[:, :], lntmp[:, :], AF.Exp, scale=-1.0, reads=kLT,
            writes=["lnsq"], tag=f"{tag}.sp2.{oc}")
        oneap, onek = cconst(p, c, 1.0)
        act(p, lnsq[:, :], lnsq[:, :], AF.Ln, bias=oneap, reads=["lnsq", onek],
            writes=["lnsq"], tag=f"{tag}.sp3.{oc}")
        act(p, lntmp[:, :], psS[:, :], AF.Relu, reads=kS, writes=kLT,
            tag=f"{tag}.sp4.{oc}")
        tt(p, c["mb_dc"][:, bb, :], lntmp[:, :], lnsq[:, :], ALU.add,
           reads=kLT + ["lnsq"], writes=[("mdc", bb)], tag=f"{tag}.sp5.{oc}")
        dma(p, "gpsimd", c["mb_dd"][oc], c["mb_dc"][:, bb, :],
            reads=[("mdc", bb)], writes=["mdd"], tag=f"{tag}.sdd.{oc}")
        ttscan(p, c["mb_h"][:, :], c["mb_o1"][:, :], c["mb_dc"][:, bb, :], 0.0,
               ALU.mult, ALU.add, reads=["mo1", ("mdc", bb)], writes=["mh"],
               tag=f"{tag}.cs.{oc}")
        vcopy(p, c["mb_sf"][:, oc:oc + 1], c["mb_h"][:, 2047:2048], reads=["mh"],
              writes=["msf"], tag=f"{tag}.sf.{oc}")
        vcopy(p, c["mb_sc2"][:, bb, :], c["mb_h"][:, :], reads=["mh"],
              writes=[("msc", bb)], tag=f"{tag}.sc.{oc}")
        dma(p, "gpsimd", c["mb_sd"][oc], c["mb_sc2"][:, bb, :],
            reads=[("msc", bb)], writes=["msd"], tag=f"{tag}.ssd.{oc}")
    for oc in range(16):
        bb = oc % 2
        kw = (f"{tag}.w", bb)
        dma(p, "gpsimd", c["mb_w"][:, bb, :, :], W["win"][oc], reads=(),
            writes=[kw], tag=f"{tag}.lwi.{oc}")
        for nb in range(NBLK):
            sl = slice(nb * 512, (nb + 1) * 512)
            for kc in range(NCH):
                matmul(p, psS[:, sl], c["mb_w"][:, bb, kc, :], xn[:, kc, sl],
                       start=(kc == 0), stop=(kc == 7),
                       reads=[kw, ("xn", kc)], writes=[("psS", nb)],
                       tag=f"{tag}.mi.{oc}.{nb}.{kc}")
        for kc in range(NCH):
            matmul(p, psO[:, 0:3], c["mb_w"][:, bb, kc, :], c["mb_hx"][:, kc, :],
                   start=(kc == 0), stop=(kc == 7), reads=[kw, "mhx"],
                   writes=[("psO", 0)], tag=f"{tag}.mh.{oc}.{kc}")
        act(p, cb[:, 1:2049], psS[:, :], AF.Identity, reads=kS, writes=["mcb"],
            tag=f"{tag}.cbe.{oc}")
        vcopy(p, cb[:, 0:1], psO[:, 0:1], reads=[("psO", 0)], writes=["mcb"],
              tag=f"{tag}.cbh1.{oc}")
        vcopy(p, cb[:, 2049:2051], psO[:, 1:3], reads=[("psO", 0)], writes=["mcb"],
              tag=f"{tag}.cbh2.{oc}")
        ts(p, lntmp[:, :], cb[:, 0:2048], c["mb_kc"][:, oc, 0:1], ALU.mult,
           reads=["mcb", "mkc"], writes=kLT, tag=f"{tag}.cv0.{oc}")
        for j in range(1, 4):
            stt(p, lntmp[:, :], cb[:, j:2048 + j], c["mb_kc"][:, oc, j:j + 1],
                lntmp[:, :], ALU.mult, ALU.add, reads=["mcb", "mkc"] + kLT,
                writes=kLT, tag=f"{tag}.cv{j}.{oc}")
        act(p, c["mb_uc"][:, bb, :], lntmp[:, :], AF.Silu,
            bias=c["mb_bc"][:, oc:oc + 1], reads=kLT + ["mbc"],
            writes=[("muc", bb)], tag=f"{tag}.usl.{oc}")
        dma(p, "gpsimd", c["mb_ud"][oc], c["mb_uc"][:, bb, :],
            reads=[("muc", bb)], writes=["mud"], tag=f"{tag}.sud.{oc}")
    for oc in range(16, 32):
        bb = oc % 2
        kw = (f"{tag}.w", bb)
        dma(p, "gpsimd", c["mb_w"][:, bb, :, :], W["win"][oc], reads=(),
            writes=[kw], tag=f"{tag}.lwr.{oc}")
        for nb in range(NBLK):
            sl = slice(nb * 512, (nb + 1) * 512)
            for kc in range(NCH):
                matmul(p, psS[:, sl], c["mb_w"][:, bb, kc, :], xn[:, kc, sl],
                       start=(kc == 0), stop=(kc == 7),
                       reads=[kw, ("xn", kc)], writes=[("psS", nb)],
                       tag=f"{tag}.mr.{oc}.{nb}.{kc}")
        act(p, c["mb_sr"][:, bb, :], psS[:, :], AF.Silu, reads=kS,
            writes=[("msr", bb)], tag=f"{tag}.rsl.{oc}")
        dma(p, "gpsimd", c["mb_srd"][oc - 16], c["mb_sr"][:, bb, :],
            reads=[("msr", bb)], writes=["msrd"], tag=f"{tag}.ssr.{oc}")
    for g in range(16):
        gb = g % 2
        dma(p, "sync", c["mb_dc"][:, gb, :], c["mb_dd"][g], reads=["mdd"],
            writes=[("mdc", gb)], tag=f"{tag}.ldd.{g}")
        dma(p, "sync", c["mb_uc"][:, gb, :], c["mb_ud"][g], reads=["mud"],
            writes=[("muc", gb)], tag=f"{tag}.lud.{g}")
        tt(p, c["mb_du"][:, :], c["mb_dc"][:, gb, :], c["mb_uc"][:, gb, :],
           ALU.mult, reads=[("mdc", gb), ("muc", gb)], writes=["mdu"],
           tag=f"{tag}.du.{g}")
        for j in range(16):
            k = g * 16 + j
            for nb in range(NBLK):
                sl = slice(nb * 512, (nb + 1) * 512)
                matmul(p, psS[:, sl], c["mb_sel"][:, j, :], c["mb_dc"][:, gb, sl],
                       start=True, stop=True, reads=["msel", ("mdc", gb)],
                       writes=[("psS", nb)], tag=f"{tag}.dr.{k}.{nb}")
            act(p, c["mb_da"][:, :], psS[:, :], AF.Exp,
                scale=c["mb_A"][:, k:k + 1], reads=kS + ["mA"], writes=["mda"],
                tag=f"{tag}.dae.{k}")
            for nb in range(NBLK):
                sl = slice(nb * 512, (nb + 1) * 512)
                matmul(p, psS[:, sl], c["mb_sel"][:, j, :], c["mb_du"][:, sl],
                       start=True, stop=True, reads=["msel", "mdu"],
                       writes=[("psS", nb)], tag=f"{tag}.ur.{k}.{nb}")
            tt(p, c["mb_dbu"][:, :], psS[:, :], c["mb_br"][:, :], ALU.mult,
               reads=kS + ["mbr"], writes=["mdbu"], tag=f"{tag}.dbu.{k}")
            ttscan(p, c["mb_h"][:, :], c["mb_da"][:, :], c["mb_dbu"][:, :], 0.0,
                   ALU.mult, ALU.add, reads=["mda", "mdbu"], writes=["mh"],
                   tag=f"{tag}.scan.{k}")
            vcopy(p, c["mb_hf"][:, 0, k:k + 1], c["mb_h"][:, 2047:2048],
                  reads=["mh"], writes=["mhf"], tag=f"{tag}.hf.{k}")
            tt(p, c["mb_hc"][:, :], c["mb_h"][:, :], c["mb_cr"][:, :], ALU.mult,
               reads=["mh", "mcr"], writes=["mhc"], tag=f"{tag}.hc.{k}")
            for nb in range(NBLK):
                sl = slice(nb * 512, (nb + 1) * 512)
                matmul(p, psO[:, sl], c["mb_red"][:, j, :], c["mb_hc"][:, sl],
                       start=(j == 0), stop=(j == 15), reads=["mred", "mhc"],
                       writes=[("psO", nb)], tag=f"{tag}.yr.{k}.{nb}")
        vcopy(p, c["mb_sc2"][:, gb, :], psO[:, :], reads=kO, writes=[("msc", gb)],
              tag=f"{tag}.y1e.{g}")
        dma(p, "gpsimd", c["mb_y1"][g], c["mb_sc2"][:, gb, :],
            reads=[("msc", gb)], writes=["my1"], tag=f"{tag}.sy1.{g}")
    for g in range(16):
        for j in range(16):
            matmul(p, psS[:, g * 16 + j:g * 16 + j + 1], c["mb_sel"][:, j, :],
                   c["mb_sf"][:, g:g + 1], start=True, stop=True,
                   reads=["msel", "msf"], writes=[("psS", (g * 16 + j) // 512)],
                   tag=f"{tag}.str.{g}.{j}")
    tt(p, c["mb_hin"][:, :], psS[:, 0:256], c["mb_A"][:, :], ALU.mult,
       reads=[("psS", 0)] + ["mA"], writes=["mhin"], tag=f"{tag}.ap1")
    act(p, c["mb_hf"][:, 1, :], c["mb_hin"][:, :], AF.Exp, reads=["mhin"],
        writes=["mhf"], tag=f"{tag}.ap2")
    dma(p, "gpsimd", c["mb_hsrc"], c["mb_hf"][:, :, :].rearrange("p a b -> p (a b)"),
        reads=["mhf"], writes=["mhsrc"], tag=f"{tag}.shf")
    collective(p, "AllGather", ALU.bypass, grp, ins=[c["mb_hsrc"]],
               outs=[c["mb_hg"]], reads=["mhsrc"], writes=["mhg"], tag=f"{tag}.gh")
    for r in range(4):
        dma(p, "sync", c["mb_ch"][:, r, :], c["mb_hg"][r], reads=["mhg"],
            writes=["mch"], tag=f"{tag}.lhg{r}")
    H = lambda r: c["mb_ch"][:, r, 0:256]
    A_ = lambda r: c["mb_ch"][:, r, 256:512]
    h2, h3 = c["mb_hf"][:, 0, :], c["mb_hf"][:, 1, :]
    tt(p, c["mb_hin"][:, :], A_(1), H(0), ALU.mult, reads=["mch"],
       writes=["mhin"], tag=f"{tag}.pc1")
    tt(p, h2, c["mb_hin"][:, :], H(1), ALU.add, reads=["mch", "mhin"],
       writes=["mhf"], tag=f"{tag}.pc2")
    tt(p, c["mb_hin"][:, :], A_(2), h2, ALU.mult, reads=["mch", "mhf"],
       writes=["mhin"], tag=f"{tag}.pc3")
    tt(p, h3, c["mb_hin"][:, :], H(2), ALU.add, reads=["mch", "mhin"],
       writes=["mhf"], tag=f"{tag}.pc4")
    ts(p, c["mb_hin"][:, :], H(0), c["mb_rs"][:, 1:2], ALU.mult,
       reads=["mch", "mrs"], writes=["mhin"], tag=f"{tag}.ps1")
    stt(p, c["mb_hin"][:, :], h2, c["mb_rs"][:, 2:3], c["mb_hin"][:, :],
        ALU.mult, ALU.add, reads=["mhf", "mrs", "mhin"], writes=["mhin"],
        tag=f"{tag}.ps2")
    stt(p, c["mb_hin"][:, :], h3, c["mb_rs"][:, 3:4], c["mb_hin"][:, :],
        ALU.mult, ALU.add, reads=["mhf", "mrs", "mhin"], writes=["mhin"],
        tag=f"{tag}.ps3")
    for g in range(16):
        gb = g % 2
        dma(p, "sync", c["mb_sc2"][:, gb, :], c["mb_sd"][g], reads=["msd"],
            writes=[("msc", gb)], tag=f"{tag}.lsd.{g}")
        dma(p, "sync", c["mb_uc"][:, gb, :], c["mb_ud"][g], reads=["mud"],
            writes=[("muc", gb)], tag=f"{tag}.lud2.{g}")
        dma(p, "sync", c["mb_sr"][:, gb, :], c["mb_srd"][g], reads=["msrd"],
            writes=[("msr", gb)], tag=f"{tag}.lsr.{g}")
        dma(p, "sync", c["mb_dc"][:, gb, :], c["mb_y1"][g], reads=["my1"],
            writes=[("mdc", gb)], tag=f"{tag}.ly1.{g}")
        for j in range(16):
            k = g * 16 + j
            for nb in range(NBLK):
                sl = slice(nb * 512, (nb + 1) * 512)
                matmul(p, psS[:, sl], c["mb_sel"][:, j, :], c["mb_sc2"][:, gb, sl],
                       start=True, stop=True, reads=["msel", ("msc", gb)],
                       writes=[("psS", nb)], tag=f"{tag}.s2r.{k}.{nb}")
            act(p, c["mb_da"][:, :], psS[:, :], AF.Exp,
                scale=c["mb_A"][:, k:k + 1], reads=kS + ["mA"], writes=["mda"],
                tag=f"{tag}.pe.{k}")
            stt(p, c["mb_hc"][:, :], c["mb_da"][:, :], c["mb_hin"][:, k:k + 1],
                c["mb_cr"][:, :], ALU.mult, ALU.mult,
                reads=["mda", "mhin", "mcr"], writes=["mhc"], tag=f"{tag}.cor.{k}")
            for nb in range(NBLK):
                sl = slice(nb * 512, (nb + 1) * 512)
                matmul(p, psO[:, sl], c["mb_red"][:, j, :], c["mb_hc"][:, sl],
                       start=(j == 0), stop=(j == 15), reads=["mred", "mhc"],
                       writes=[("psO", nb)], tag=f"{tag}.cr.{k}.{nb}")
        tt(p, lntmp[:, :], psO[:, :], c["mb_dc"][:, gb, :], ALU.add,
           reads=kO + [("mdc", gb)], writes=kLT, tag=f"{tag}.ya.{g}")
        stt(p, lntmp[:, :], c["mb_uc"][:, gb, :], c["mb_D"][:, g:g + 1],
            lntmp[:, :], ALU.mult, ALU.add, reads=[("muc", gb), "mD"] + kLT,
            writes=kLT, tag=f"{tag}.yd.{g}")
        tt(p, c["mb_hc"][:, :], lntmp[:, :], c["mb_sr"][:, gb, :], ALU.mult,
           reads=kLT + [("msr", gb)], writes=["mhc"], tag=f"{tag}.yg.{g}")
        dma(p, "gpsimd", c["mb_yo"][g], c["mb_hc"][:, :], reads=["mhc"],
            writes=["myo"], tag=f"{tag}.syo.{g}")
    xin_r = c["xin_dram"].rearrange("(c p) t -> p c t", p=128)
    xout_r = c["xout_dram"].rearrange("(c p) t -> p c t", p=128)
    woall = c["mb_w"][:, :, :, :].rearrange("p a b c -> p (a b c)").rearrange(
        "p (k j) -> p k j", k=16)
    for oc in range(8):
        kw = (f"{tag}.wo",)
        dma(p, "gpsimd", woall, W["wout"][oc], reads=(), writes=[kw],
            tag=f"{tag}.lwo.{oc}")
        for kc in range(16):
            gb2 = kc % 2
            dma(p, "sync", c["mb_sc2"][:, gb2, :], c["mb_yo"][kc], reads=["myo"],
                writes=[("msc", gb2)], tag=f"{tag}.lyo.{oc}.{kc}")
            for nb in range(NBLK):
                sl = slice(nb * 512, (nb + 1) * 512)
                matmul(p, psS[:, sl], woall[:, kc, :],
                       c["mb_sc2"][:, gb2, sl], start=(kc == 0), stop=False,
                       reads=[kw, ("msc", gb2)], writes=[("psS", nb)],
                       tag=f"{tag}.mo.{oc}.{kc}.{nb}")
        for nb in range(NBLK):
            sl = slice(nb * 512, (nb + 1) * 512)
            matmul(p, psS[:, sl], c["mb_bout"][0:1, oc * 128:(oc + 1) * 128],
                   c["onesrow"][0:1, 0:512], start=False, stop=True,
                   reads=["mbout", "onesrow"], writes=[("psS", nb)],
                   tag=f"{tag}.mb.{oc}.{nb}")
            dma(p, "sync", c["xs"][:, nb % 2, 0:512], xin_r[:, oc, sl], reads=[],
                writes=[("xs", nb % 2)], tag=f"{tag}.rlx.{oc}.{nb}")
            tt(p, c["xob"][:, nb % 2, :], psS[:, sl], c["xs"][:, nb % 2, 0:512],
               ALU.add, reads=[("psS", nb), ("xs", nb % 2)],
               writes=[("xob", nb % 2)], tag=f"{tag}.res.{oc}.{nb}")
            dma(p, "sync", xout_r[:, oc, sl], c["xob"][:, nb % 2, :],
                reads=[("xob", nb % 2)], writes=[], tag=f"{tag}.rsx.{oc}.{nb}")

def prep_l1ab(inputs, xs_halo):
    W = {}
    W["win"] = _swz(inputs["L1_attn_b_W_in"])
    W["wout"] = _swz(inputs["L1_attn_b_W_out"])
    W["wx"] = np.ascontiguousarray(
        inputs["L1_attn_b_W_x"].reshape(8, 128, 96))
    W["wdt"] = np.ascontiguousarray(inputs["L1_attn_b_W_dt"].reshape(64, 16, 128))
    W["bdt"] = np.ascontiguousarray(inputs["L1_attn_b_b_dt"][None].astype(np.float32))
    W["bout"] = np.ascontiguousarray(inputs["L1_attn_b_b_out"][None].astype(np.float32))
    AL = inputs["L1_attn_b_A_log"]
    Acol = np.zeros((128, 256), np.float32)
    for k in range(256):
        for pgrp in range(8):
            Acol[pgrp * 16:(pgrp + 1) * 16, k] = AL[8 * k + pgrp, :]
    W["Acol"] = Acol
    KC = inputs["L1_attn_b_K_conv"][:, 0, :]
    W["kconv"] = np.ascontiguousarray(KC.T.reshape(16, 128, 4).transpose(1, 0, 2).astype(np.float32))
    W["bconv"] = np.ascontiguousarray(
        inputs["L1_attn_b_b_conv"].reshape(16, 128).T.astype(np.float32))
    W["Dcol"] = np.ascontiguousarray(
        inputs["L1_attn_b_D"].reshape(16, 128).T.astype(np.float32))
    selk = np.zeros((128, 16, 128), np.float32)
    redk = np.zeros((128, 16, 128), np.float32)
    for j in range(16):
        for pp in range(128):
            dd = 8 * j + pp // 16
            selk[dd, j, pp] = 1.0
            redk[pp, j, dd] = 1.0
    W["selk"] = selk
    W["redk"] = redk
    p16 = np.zeros((16, 128), np.float32)
    for pp in range(128):
        p16[pp % 16, pp] = 1.0
    W["pat16"] = p16
    W["rsel_per_core"] = []
    for qr in range(4):
        rs = np.zeros((128, 4), np.float32)
        rs[:, qr] = 1.0
        W["rsel_per_core"].append(rs)
    W["xhalo_per_core"] = xs_halo
    return W

MB_W_SHAPES = {
    "win": ([32, 128, 8, 128], F32), "wout": ([8, 128, 16, 128], F32),
    "wx": ([8, 128, 96], F32), "wdt": ([64, 16, 128], F32),
    "bdt": ([1, 2048], F32), "bout": ([1, 1024], F32),
    "Acol": ([128, 256], F32), "kconv": ([128, 16, 4], F32),
    "bconv": ([128, 16], F32), "Dcol": ([128, 16], F32),
    "selk": ([128, 16, 128], F32), "redk": ([128, 16, 128], F32),
    "pat16": ([16, 128], F32), "rsel": ([128, 4], F32),
    "xhalo": ([128, 8, 3], F32),
}


def _bcl(a):
    n = a.shape[0]
    return np.ascontiguousarray(a.reshape(n // 128, 128).T.astype(np.float32))

def _swz(WT):
    K, O = WT.shape
    return np.ascontiguousarray(
        WT.reshape(K // 128, 128, O // 128, 128).transpose(2, 1, 0, 3))

def prep_l0ffn(inputs):
    W = {}
    W["w1"] = _swz(inputs["L0_ffn_w1_w"].T)
    W["w2"] = _swz(inputs["L0_ffn_w2_w"].T)
    W["h1"] = _swz(inputs["L0_ffn_h1_w"].T)
    W["h2"] = _swz(inputs["L0_ffn_h2_w"].T)
    W["w3"] = _swz(inputs["L0_ffn_w3_w"].T)
    W["w1_s"] = _bcl(inputs["L0_ffn_w1_s"])
    W["w2_s"] = _bcl(inputs["L0_ffn_w2_s"])
    W["h1_s"] = _bcl(inputs["L0_ffn_h1_s"])
    W["h2_s"] = _bcl(inputs["L0_ffn_h2_s"])
    W["w3_s"] = _bcl(inputs["L0_ffn_w3_s"])
    return W

FFN_W_SHAPES = {
    "w1": ([32, 128, 8, 128], F32), "w2": ([32, 128, 8, 128], F32),
    "h1": ([32, 128, 32, 128], F32), "h2": ([32, 128, 32, 128], F32),
    "w3": ([8, 128, 32, 128], F32),
    "w1_s": ([128, 32], F32), "w2_s": ([128, 32], F32),
    "h1_s": ([128, 32], F32), "h2_s": ([128, 32], F32), "w3_s": ([128, 8], F32),
}


def build_sublayer_nc(name):
    nc = bass.Bass("TRN2", target_bir_lowering=False)
    es = contextlib.ExitStack()
    xin = nc.dram_tensor("xin", [D, TLOC], F32, kind="ExternalInput").ap()
    xout = nc.dram_tensor("xout", [D, TLOC], F32, kind="ExternalOutput").ap()
    dident = nc.dram_tensor("d_ident", [128, 128], F32, kind="ExternalInput").ap()
    dW = {}
    if name in ("l0ffn1", "l0ffn2"):
        for k, (shape, dt) in FFN_W_SHAPES.items():
            dW[k] = nc.dram_tensor("w_" + k, shape, dt, kind="ExternalInput").ap()
    if name in ("l1moe1", "l1moe2"):
        for k, (shape, dt) in MOE_W_SHAPES.items():
            dW[k] = nc.dram_tensor("w_" + k, shape, dt, kind="ExternalInput").ap()
    if name == "l0aa":
        for k, (shape, dt) in AA_W_SHAPES.items():
            dW[k] = nc.dram_tensor("w_" + k, shape, dt, kind="ExternalInput").ap()
    if name == "l0ab":
        for k, (shape, dt) in AB_W_SHAPES.items():
            dW[k] = nc.dram_tensor("w_" + k, shape, dt, kind="ExternalInput").ap()
    if name == "l1aa":
        for k, (shape, dt) in LA_W_SHAPES.items():
            dW[k] = nc.dram_tensor("w_" + k, shape, dt, kind="ExternalInput").ap()
    if name == "l1ab":
        for k, (shape, dt) in MB_W_SHAPES.items():
            dW[k] = nc.dram_tensor("w_" + k, shape, dt, kind="ExternalInput").ap()
    resident = name not in ("l1moe1", "l1moe2", "l0aa", "l0ab", "l1aa", "l1ab")
    p = Prog(nc)
    c = make_ctx(nc, es, resident_x=resident)
    c["xin_dram"], c["xout_dram"] = xin, xout
    load_ctx_constants(p, c, dident)
    if resident:
        dma(p, "sync", c["x"][:, :, :], xin.rearrange("(c p) t -> p c t", p=128),
            reads=(), writes=["x"], tag="ld_x")
    emit_ln(p, c, "ln", xin=xin)
    if name in ("l0ffn1", "l0ffn2"):
        emit_l0ffn(p, c, es, dW, "ffn")
    if name in ("l1moe1", "l1moe2"):
        emit_l1moe(p, c, es, dW, "moe")
    if name == "l0aa":
        emit_l0aa(p, c, es, dW, "aa")
    if name == "l0ab":
        emit_l0ab(p, c, es, dW, "ab")
    if name == "l1aa":
        emit_l1aa(p, c, es, dW, "la")
    if name == "l1ab":
        emit_l1ab(p, c, es, dW, "mb")
    if resident:
        fins = [dma(p, "sync", xout.rearrange("(c p) t -> p c t", p=128),
                    c["x"][:, :, :], reads=["x"], writes=[], tag="st_x")]
    else:
        fins = [op for op in p.ops if op.kind == "dma" and ".rsx." in op.tag]
        assert fins
    if os.environ.get("MK_DUMP") and name == "l0aa":
        dq = nc.dram_tensor("dump_q", [128, 8, 2048], F32, kind="ExternalOutput").ap()
        dm = nc.dram_tensor("dump_m", [128, 16], F32, kind="ExternalOutput").ap()
        dy = nc.dram_tensor("dump_y", [128, 8, 2048], F32, kind="ExternalOutput").ap()
        dn = nc.dram_tensor("dump_n", [16, 2048], F32, kind="ExternalOutput").ap()
        fins.append(dma(p, "gpsimd", dq, c["aa_q"][:, :, :],
                        reads=[("aaq", i) for i in range(8)], writes=[], tag="dmpq"))
        fins.append(dma(p, "gpsimd", dm, c["aa_m"][:, :], reads=["aam"], writes=[], tag="dmpm"))
        fins.append(dma(p, "gpsimd", dy, c["xn"][:, :, :],
                        reads=[("aay", i) for i in range(8)], writes=[], tag="dmpy"))
        fins.append(dma(p, "gpsimd", dn, c["aa_nrm"], reads=["kn2s"], writes=[], tag="dmpn"))
        drw = nc.dram_tensor("dump_rows", [128, 2048], F32, kind="ExternalOutput").ap()
        fins.append(dma(p, "gpsimd", drw, c["rowbuf"][:, :],
                        reads=["kmx", "qm2x", "mh2", "mh", "mhrow", "rz", "r0"],
                        writes=[], tag="dmpr"))
    if os.environ.get("MK_DUMP") and name in ("l1moe1", "l1moe2"):
        dsc = nc.dram_tensor("dump_sc", [16, 2048], F32, kind="ExternalOutput").ap()
        dcf = nc.dram_tensor("dump_cf", [16, 2048], F32, kind="ExternalOutput").ap()
        dtm = nc.dram_tensor("dump_tm", [128, 16], F32, kind="ExternalOutput").ap()
        dH = nc.dram_tensor("dump_H", [128, 4, 512], F32, kind="ExternalOutput").ap()
        fins.append(dma(p, "gpsimd", dsc, c["lntmp"][0:16, 0:2048],
                        reads=["lntmp", "r3", "r2", "r0sq"], writes=[], tag="dmp1"))
        fins.append(dma(p, "gpsimd", dcf, c["moe_cf"][:, :], reads=["cf"], writes=[], tag="dmp2"))
        fins.append(dma(p, "gpsimd", dtm, c["moe_sc"][:, 0, :], reads=[("sctm", 0)], writes=[], tag="dmp3"))
        fins.append(dma(p, "gpsimd", dH, c["moe_H"][:, 3, :, :], reads=[("H", 3, 0), ("H", 3, 1), ("H", 3, 2), ("H", 3, 3)], writes=[], tag="dmp4"))
        drs = nc.dram_tensor("dump_rs", [128, 128], F32, kind="ExternalOutput").ap()
        fins.append(dma(p, "gpsimd", drs, c["moe_rs"][:, :], reads=["rs"], writes=[], tag="dmp5"))
        ddbg = nc.dram_tensor("dump_dbg", [128, 3, 128], F32, kind="ExternalOutput").ap()
        fins.append(dma(p, "gpsimd", ddbg, c["moe_dbg"][:, :, :], reads=["dbg"], writes=[], tag="dmp6"))
    p.build(final_ops=fins)
    return nc, es

def make_halos(xs):
    halos = []
    for cidx in range(NCORES):
        b, qr = cidx // 4, cidx % 4
        h = np.zeros((1024, 3), np.float32)
        if qr > 0:
            h[:, 0] = xs[cidx - 1][:, 2047]
        if qr < 3:
            h[:, 1] = xs[cidx + 1][:, 0]
            h[:, 2] = xs[cidx + 1][:, 1]
        halos.append(h)
    return halos

def run_sublayer(name, xs, inputs_prepped):
    if name == "l1ab":
        inputs_prepped["l1ab"]["xhalo_per_core"] = make_halos(xs)
    nc, es = build_sublayer_nc(name)
    ident = np.eye(128, dtype=np.float32)
    in_maps = []
    for cidx in range(NCORES):
        m = {"xin": xs[cidx], "d_ident": ident}
        if name in ("l0ffn1", "l0ffn2"):
            for k in FFN_W_SHAPES:
                m["w_" + k] = inputs_prepped["l0ffn"][k]
        if name in ("l1moe1", "l1moe2"):
            for k in MOE_W_SHAPES:
                m["w_" + k] = inputs_prepped["l1moe"][k]
        if name == "l0aa":
            Wa = inputs_prepped["l0aa"]
            for k in AA_W_SHAPES:
                if k in ("cos", "sin"):
                    m["w_" + k] = Wa[k + "_per_core"][cidx % 4]
                else:
                    m["w_" + k] = Wa[k]
        if name == "l0ab":
            for k in AB_W_SHAPES:
                m["w_" + k] = inputs_prepped["l0ab"][k]
        if name == "l1aa":
            Wl = inputs_prepped["l1aa"]
            for k in LA_W_SHAPES:
                m["w_" + k] = Wl["wrow_per_core"][cidx // 4] if k == "wrow" else Wl[k]
        if name == "l1ab":
            Wm = inputs_prepped["l1ab"]
            for k in MB_W_SHAPES:
                if k == "rsel":
                    m["w_" + k] = Wm["rsel_per_core"][cidx % 4]
                elif k == "xhalo":
                    h = Wm["xhalo_per_core"][cidx]
                    m["w_" + k] = np.ascontiguousarray(
                        h.reshape(8, 128, 3).transpose(1, 0, 2))
                else:
                    m["w_" + k] = Wm[k]
        in_maps.append(m)
    res = run_bass_kernel_spmd(nc, in_maps, core_ids=list(range(NCORES)))
    es.close()
    return [res.results[i]["xout"] for i in range(NCORES)]


SUBLAYERS = ["l0aa", "l0ffn1", "l0ab", "l0ffn2", "l1aa", "l1moe1", "l1ab", "l1moe2"]

def kernel(**inputs):
    x = inputs["L0_attn_a_x"]
    xs = []
    for cidx in range(NCORES):
        b, qr = cidx // 4, cidx % 4
        xs.append(np.ascontiguousarray(x[b, qr * TLOC:(qr + 1) * TLOC, :].T))
    prepped = {"l0ffn": prep_l0ffn(inputs), "l1moe": prep_l1moe(inputs),
               "l0aa": prep_l0aa(inputs, [0, 2048, 4096, 6144]),
               "l0ab": prep_l0ab(inputs), "l1aa": prep_l1aa(inputs),
               "l1ab": prep_l1ab(inputs, None)}
    for name in SUBLAYERS:
        xs = run_sublayer(name, xs, prepped)
    out = np.empty((2, T_GLOB, D), np.float32)
    for cidx in range(NCORES):
        b, qr = cidx // 4, cidx % 4
        out[b, qr * TLOC:(qr + 1) * TLOC, :] = xs[cidx].T
    return out
